# version 02 (final)
# baseline (speedup 1.0000x reference)
import numpy as np
import ml_dtypes
import concourse.bass as bass
import concourse.mybir as mybir
from concourse.bass_utils import run_bass_kernel_spmd

F32 = mybir.dt.float32
BF16 = mybir.dt.bfloat16
AF = mybir.ActivationFunctionType
ALU = mybir.AluOpType
AX = mybir.AxisListType

M = 4096
DM = 1024
DI = 2048
DS = 16
NP = 2048
NEXP, CAP = 64, 32
NH, HD = 16, 64
LN_EPS = 1e-5
EXP_FLOOR = float(np.float32(np.exp(np.float64(-20.0))))


class S:
    def __init__(self, nc, name):
        self._cm = nc.semaphore(name)
        self.h = self._cm.__enter__()
        self.v = 0


def bump(sem, instr, n=1):
    sem.v += n
    return instr.then_inc(sem.h, n)


def bnop(sem, eng, n=1):
    eng.drain()
    return bump(sem, eng.nop(), n)



_EXPORT_X1TOK = True
_EXPORT_ELT = True


def build_L1(debug=()):
    nc = bass.Bass("TRN2", target_bir_lowering=False, detect_race_conditions=False)

    xT = nc.dram_tensor("xT", [DM, M], F32, kind="ExternalInput").ap()
    w_inproj = nc.dram_tensor("w_inproj", [DM, 2 * DI], BF16, kind="ExternalInput").ap()
    w_conv = nc.dram_tensor("w_conv", [DI, 4], F32, kind="ExternalInput").ap()
    conv_b = nc.dram_tensor("conv_b", [DI, 1], F32, kind="ExternalInput").ap()
    w_xproj = nc.dram_tensor("w_xproj", [DI, 33], BF16, kind="ExternalInput").ap()
    w_dtw = nc.dram_tensor("w_dtw", [DI, 1], F32, kind="ExternalInput").ap()
    dt_b = nc.dram_tensor("dt_b", [DI, 1], F32, kind="ExternalInput").ap()
    A_log = nc.dram_tensor("A_log", [DS, 1], F32, kind="ExternalInput").ap()
    sel16c = nc.dram_tensor("sel16c", [DS, 128], F32, kind="ExternalInput").ap()
    D_vec = nc.dram_tensor("D_vec", [DI, 1], F32, kind="ExternalInput").ap()
    w_outproj = nc.dram_tensor("w_outproj", [DI, DM], BF16, kind="ExternalInput").ap()
    ln_g_in = nc.dram_tensor("ln_g", [DI, 1], F32, kind="ExternalInput").ap()
    ln_b_in = nc.dram_tensor("ln_b", [DI, 1], F32, kind="ExternalInput").ap()
    phi = nc.dram_tensor("phi", [DM, NP], BF16, kind="ExternalInput").ap()
    mask8 = nc.dram_tensor("mask8", [8, 128], BF16, kind="ExternalInput").ap()
    maskf = nc.dram_tensor("maskf", [128, 8], F32, kind="ExternalInput").ap()
    onesr = nc.dram_tensor("onesr", [1, 128], F32, kind="ExternalInput").ap()
    onesc_b = nc.dram_tensor("onesc_b", [128, 1], BF16, kind="ExternalInput").ap()
    identf = nc.dram_tensor("identf", [128, 128], F32, kind="ExternalInput").ap()
    identb = nc.dram_tensor("identb", [128, 128], BF16, kind="ExternalInput").ap()
    onesrb_in = nc.dram_tensor("onesrb", [1, 128], BF16, kind="ExternalInput").ap()

    r1_out = nc.dram_tensor("r1", [DM, M], F32, kind="ExternalOutput").ap()
    el1_out = nc.dram_tensor("el1", [M, NP], BF16, kind="ExternalOutput").ap()
    x1lntok_out = nc.dram_tensor("x1lntok", [M, DM], BF16, kind="ExternalOutput").ap()
    el1Tsc_out = nc.dram_tensor("el1Tsc", [NP, M], BF16, kind="ExternalOutput").ap()

    z_sp = nc.dram_tensor("z_sp", [DI, M], F32, kind="Internal").ap()
    u_sp = nc.dram_tensor("u_sp", [DI, M], F32, kind="Internal").ap()
    dthi_sp = nc.dram_tensor("dthi_sp", [DI, M], BF16, kind="Internal").ap()
    dtlo_sp = nc.dram_tensor("dtlo_sp", [DI, M], BF16, kind="Internal").ap()
    duhi_sp = nc.dram_tensor("duhi_sp", [DI, M], BF16, kind="Internal").ap()
    dulo_sp = nc.dram_tensor("dulo_sp", [DI, M], BF16, kind="Internal").ap()
    xdbl_sp = nc.dram_tensor("xdbl_sp", [33, M], F32, kind="Internal").ap()
    y_sp = nc.dram_tensor("y_sp", [M, DI], F32, kind="Internal").ap()
    yn_sp = nc.dram_tensor("yn_sp", [M, DI], F32, kind="Internal").ap()
    gat_sp = nc.dram_tensor("gat_sp", [DI, M], BF16, kind="Internal").ap()

    dbg_outs = {}
    scr = {"u_sp": u_sp, "z_sp": z_sp, "xdbl_sp": xdbl_sp, "y_sp": y_sp,
           "dthi_sp": dthi_sp, "dtlo_sp": dtlo_sp, "duhi_sp": duhi_sp,
           "dulo_sp": dulo_sp, "yn_sp": yn_sp, "gat_sp": gat_sp}
    for name in debug:
        src = scr[name]
        dbg_outs[name] = (nc.dram_tensor("dbg_" + name, list(src.shape),
                                         src.dtype, kind="ExternalOutput").ap(), src)

    TCP = 128
    NPRE = M // TCP
    TC = 8
    NSC = M // TC
    TPO = 256
    NPO = M // TPO

    ctx = []

    def sb(name, shape, dt):
        cm = nc.sbuf_tensor(name, shape, dt)
        t = cm.__enter__()
        ctx.append(cm)
        return t

    def psum(name, shape, dt=F32):
        cm = nc.psum_tensor(name, shape, dt)
        t = cm.__enter__()
        ctx.append(cm)
        return t

    a16 = sb("a16", [DS, 1], F32)
    sel16_sb = sb("sel16_sb", [DS, 128], F32)
    cst_sb = sb("cst_sb", [128, 8], F32)
    A_col = sb("A_col", [128, 1], F32)
    ones_row = sb("ones_row", [1, 128], F32)
    onesc_bf = sb("onesc_bf", [128, 1], BF16)
    onesr_b = sb("onesr_b", [1, 128], BF16)
    identb_sb = sb("identb_sb", [128, 128], BF16)
    mask8_sb = sb("mask8_sb", [8, 128], BF16)
    maskf_sb = sb("maskf_sb", [128, 8], F32)
    identf_sb = sb("identf_sb", [128, 128], F32)
    wconv_sb = sb("wconv_sb", [128, 16 * 4], F32)
    convb_sb = sb("convb_sb", [128, 16], F32)
    wdtw_sb = sb("wdtw_sb", [128, 16], F32)
    dtb_sb = sb("dtb_sb", [128, 16], F32)
    dvec_sb = sb("dvec_sb", [128, 16], F32)
    lng_sb = sb("lng_sb", [128, 16], F32)
    lnb_sb = sb("lnb_sb", [128, 16], F32)


    def sb_scope():
        return len(ctx)

    def sb_free(mark):
        while len(ctx) > mark:
            ctx.pop().__exit__(None, None, None)

    ps_a = psum("ps_a", [128, 512])
    ps_b = psum("ps_b", [128, 512])
    ps_c = psum("ps_c", [128, 512])
    ps_d = psum("ps_d", [128, 512])
    ps_e = psum("ps_e", [128, 512])
    ps_f = psum("ps_f", [128, 512])
    ps_g = psum("ps_g", [128, 512])
    ps_h = psum("ps_h", [128, 512])

    sld = S(nc, "sld")
    sst = S(nc, "sst")
    s_t = S(nc, "s_t")
    s_v = S(nc, "s_v")
    s_s = S(nc, "s_s")
    s_g = S(nc, "s_g")
    s_sP = S(nc, "s_sP")
    s_tA = S(nc, "s_tA")
    s_tU = S(nc, "s_tU")
    s_sA = S(nc, "s_sA")
    s_sU = S(nc, "s_sU")
    s_y = S(nc, "s_y")
    s_ye = S(nc, "s_ye")
    s_yev = S(nc, "s_yev")
    s_sB = S(nc, "s_sB")
    s_ph = S(nc, "s_ph")

    SY = nc.sync
    V = nc.vector
    SC = nc.scalar
    T = nc.tensor
    G = nc.gpsimd

    def phase_barrier():
        for eng in (SY, V, SC, T, G):
            bnop(s_ph, eng)
        for eng in (SY, V, SC, T, G):
            eng.wait_ge(s_ph.h, s_ph.v)

    with nc.Block() as _block:
        for dst, src in [
            (a16[:], A_log), (sel16_sb[:], sel16c),
            (ones_row[:], onesr), (onesc_bf[:], onesc_b),
            (onesr_b[:], onesrb_in), (identb_sb[:], identb),
            (mask8_sb[:], mask8), (maskf_sb[:], maskf), (identf_sb[:], identf),
            (wconv_sb[:].rearrange("p (j k) -> p j k", k=4),
             w_conv.rearrange("(j p) k -> p j k", p=128)),
            (convb_sb[:], conv_b.rearrange("(j p) o -> p (j o)", p=128)),
            (wdtw_sb[:], w_dtw.rearrange("(j p) o -> p (j o)", p=128)),
            (dtb_sb[:], dt_b.rearrange("(j p) o -> p (j o)", p=128)),
            (dvec_sb[:], D_vec.rearrange("(j p) o -> p (j o)", p=128)),
            (lng_sb[:], ln_g_in.rearrange("(j p) o -> p (j o)", p=128)),
            (lnb_sb[:], ln_b_in.rearrange("(j p) o -> p (j o)", p=128)),
        ]:
            bump(sld, SY.dma_start(out=dst, in_=src, allow_slow_non_contiguous=True), 16)
        for ii, cv in enumerate([LN_EPS, 1.0, -1.0, -0.5, 0.0, 1e-4, 10.0, 100.0]):
            V.memset(cst_sb[:, ii:ii + 1], cv)
            nc.const_aps.register(F32, cv, cst_sb[:, ii:ii + 1])
        bnop(s_v, V)
        SC.wait_ge(s_v.h, s_v.v)
        T.wait_ge(s_v.h, s_v.v)
        G.wait_ge(s_v.h, s_v.v)
        V.wait_ge(sld.h, sld.v)
        V.tensor_scalar(out=a16[:], in0=a16[:], scalar1=5.0, scalar2=None, op0=ALU.min)
        bnop(s_v, V)
        SC.wait_ge(s_v.h, s_v.v)
        SC.activation(out=a16[:], in_=a16[:], func=AF.Exp)
        bnop(s_s, SC)
        V.wait_ge(s_s.h, s_s.v)
        V.tensor_scalar(out=a16[:], in0=a16[:], scalar1=-1.0, scalar2=None, op0=ALU.mult)
        bnop(s_v, V)
        T.wait_ge(s_v.h, s_v.v)
        ins = T.matmul(ps_a[:, 0:1], sel16_sb[:], a16[:], start=True, stop=True)
        bump(s_t, ins)
        V.wait_ge(s_t.h, s_t.v)
        V.tensor_copy(A_col[:], ps_a[:, 0:1])
        bnop(s_v, V)

        pre_mark = sb_scope()
        w_in_sb = sb("w_in_sb", [128, 8 * 2 * DI], BF16)
        w_xp_sb = sb("w_xp_sb", [128, 16 * 33], BF16)
        xc = sb("xc", [128, 8 * TCP], F32)
        xcb = sb("xcb", [128, 8 * TCP], BF16)
        xsq_pre = sb("xsq_pre", [128, 8 * TCP], BF16)
        xlnb = sb("xlnb", [128, 8 * TCP], BF16)
        xin = sb("xin", [128, 16 * (TCP + 4)], F32)
        uf = sb("uf", [128, 16 * TCP], F32)
        ub = sb("ub", [128, 16 * TCP], BF16)
        zf = sb("zf", [128, 16 * TCP], F32)
        dtf = sb("dtf", [128, 16 * TCP], F32)
        duf = sb("duf", [128, 16 * TCP], F32)
        hib = sb("hib", [128, 16 * TCP], BF16)
        lob = sb("lob", [128, 16 * TCP], BF16)
        hib2 = sb("hib2", [128, 16 * TCP], BF16)
        lob2 = sb("lob2", [128, 16 * TCP], BF16)
        hi32 = sb("hi32", [128, 16 * TCP], F32)
        spa = sb("spa", [128, 16 * TCP], F32)
        xdblf = sb("xdblf", [33, TCP], F32)
        r_sx = sb("r_sx", [1, TCP], F32)
        r_mu = sb("r_mu", [1, TCP], F32)
        r_var = sb("r_var", [1, TCP], F32)
        r_rstd = sb("r_rstd", [1, TCP], F32)
        r_mur = sb("r_mur", [1, TCP], F32)
        SY.wait_ge(sld.h, sld.v)
        bump(sld, SY.dma_start(out=w_in_sb[:].rearrange("p (g c) -> p g c", g=8),
                               in_=w_inproj.rearrange("(g p) c -> p g c", p=128)), 16)
        bump(sld, SY.dma_start(out=w_xp_sb[:].rearrange("p (g c) -> p g c", g=16),
                               in_=w_xproj.rearrange("(g p) c -> p g c", p=128)), 16)
        xT_v8 = xT.rearrange("(g p) t -> p g t", p=128)
        zsp_v = z_sp.rearrange("(j p) t -> p j t", p=128)
        usp_v = u_sp.rearrange("(j p) t -> p j t", p=128)
        dthi_v = dthi_sp.rearrange("(j p) t -> p j t", p=128)
        dtlo_v = dtlo_sp.rearrange("(j p) t -> p j t", p=128)
        duhi_v = duhi_sp.rearrange("(j p) t -> p j t", p=128)
        dulo_v = dulo_sp.rearrange("(j p) t -> p j t", p=128)
        w_in_v = w_in_sb[:].rearrange("p (g c) -> p g c", g=8)
        w_xp_v = w_xp_sb[:].rearrange("p (g c) -> p g c", g=16)
        xc_v = xc[:].rearrange("p (g t) -> p g t", g=8)
        xcb_v = xcb[:].rearrange("p (g t) -> p g t", g=8)
        xsq_v = xsq_pre[:].rearrange("p (g t) -> p g t", g=8)
        xln_v = xlnb[:].rearrange("p (g t) -> p g t", g=8)
        xin_v = xin[:].rearrange("p (j t) -> p j t", j=16)
        zf_v = zf[:].rearrange("p (b t) -> p b t", b=16)
        uf_v = uf[:].rearrange("p (j t) -> p j t", j=16)
        ub_v = ub[:].rearrange("p (j t) -> p j t", j=16)
        dtf_v = dtf[:].rearrange("p (j t) -> p j t", j=16)
        duf_v = duf[:].rearrange("p (j t) -> p j t", j=16)
        spa_v = spa[:].rearrange("p (j t) -> p j t", j=16)
        hib_v = hib[:].rearrange("p (j t) -> p j t", j=16)
        lob_v = lob[:].rearrange("p (j t) -> p j t", j=16)
        hi32_v = hi32[:].rearrange("p (j t) -> p j t", j=16)

        v_xln_done = 0
        t_inproj_done = 0
        v_dt_done = 0
        g_conv_done = 0
        v_dtdu_done = 0
        zdma_val = {}
        V.memset(xin[:], 0.0)
        bnop(s_v, V)

        for ci in range(NPRE):
            t0 = ci * TCP
            SY.wait_ge(sld.h, sld.v)
            SY.wait_ge(sst.h, sst.v)
            if ci > 0:
                SY.wait_ge(s_v.h, v_xln_done)
            bump(sld, SY.dma_start(out=xc_v, in_=xT_v8[:, :, t0:t0 + TCP]), 16)
            V.wait_ge(sld.h, sld.v)
            if ci > 0:
                V.wait_ge(s_t.h, t_inproj_done)
                SC.wait_ge(sst.h, z_dma_done)
            V.tensor_copy(xcb[:], xc[:])
            bnop(s_v, V)
            SC.wait_ge(s_v.h, s_v.v)
            SC.activation(out=xsq_pre[:], in_=xc[:], func=AF.Square)
            bnop(s_s, SC)
            T.wait_ge(s_s.h, s_s.v)
            for g in range(8):
                T.matmul(ps_a[0:1, 0:TCP], onesc_bf[:], xcb_v[:, g, :],
                         start=(g == 0), stop=(g == 7))
            for g in range(8):
                ins = T.matmul(ps_a[0:1, 256:256 + TCP], onesc_bf[:], xsq_v[:, g, :],
                               start=(g == 0), stop=(g == 7))
            bump(s_t, ins)
            V.wait_ge(s_t.h, s_t.v)
            V.tensor_scalar(out=r_mu[:], in0=ps_a[0:1, 0:TCP], scalar1=1.0 / DM,
                            scalar2=None, op0=ALU.mult)
            V.tensor_scalar(out=r_var[:], in0=ps_a[0:1, 256:256 + TCP],
                            scalar1=1.0 / DM, scalar2=None, op0=ALU.mult)
            V.tensor_tensor(out=r_sx[:], in0=r_mu[:], in1=r_mu[:], op=ALU.mult)
            V.tensor_tensor(out=r_var[:], in0=r_var[:], in1=r_sx[:], op=ALU.subtract)
            bnop(s_v, V)
            SC.wait_ge(s_v.h, s_v.v)
            SC.activation(out=r_var[:], in_=r_var[:], func=AF.Relu)
            SC.activation(out=r_var[:], in_=r_var[:], func=AF.Ln, bias=LN_EPS)
            SC.activation(out=r_rstd[:], in_=r_var[:], func=AF.Exp, scale=-0.5)
            bnop(s_s, SC)
            V.wait_ge(s_s.h, s_s.v)
            V.tensor_tensor(out=r_mur[:], in0=r_mu[:], in1=r_rstd[:], op=ALU.mult)
            bnop(s_v, V)
            T.wait_ge(s_v.h, s_v.v)
            T.matmul(ps_b[:, 0:TCP], ones_row[:], r_rstd[:], start=True, stop=True)
            ins = T.matmul(ps_b[:, 256:256 + TCP], ones_row[:], r_mur[:],
                           start=True, stop=True)
            bump(s_t, ins)
            V.wait_ge(s_t.h, s_t.v)
            for g in range(8):
                V.tensor_tensor(out=xc_v[:, g, :], in0=xc_v[:, g, :],
                                in1=ps_b[:, 0:TCP], op=ALU.mult)
                V.tensor_tensor(out=xln_v[:, g, :], in0=xc_v[:, g, :],
                                in1=ps_b[:, 256:256 + TCP], op=ALU.subtract)
            ins = bnop(s_v, V)
            v_xln_done = s_v.v

            T.wait_ge(s_v.h, v_xln_done)
            if ci > 0:
                T.wait_ge(s_v.h, g_conv_done)
            for it in range(32):
                pst = ps_c if it % 2 == 0 else ps_d
                if ci * 32 + it >= 2:
                    T.wait_ge(s_sP.h, ci * 32 + it - 1)
                for g in range(8):
                    ins = T.matmul(pst[:, 0:TCP], w_in_v[:, g, 128 * it:128 * it + 128],
                                   xln_v[:, g, :], start=(g == 0), stop=(g == 7))
                bump(s_t, ins)
                SC.wait_ge(s_t.h, s_t.v)
                if it < 16:
                    ins = SC.copy(out=xin_v[:, it, 4:4 + TCP], in_=pst[:, 0:TCP])
                else:
                    ins = SC.copy(out=zf_v[:, it - 16, :], in_=pst[:, 0:TCP])
                bump(s_sP, ins)
            SY.wait_ge(s_sP.h, s_sP.v)
            bump(sst, SY.dma_start(out=zsp_v[:, :, t0:t0 + TCP], in_=zf_v), 16)
            t_inproj_done = s_t.v
            z_dma_done = sst.v

            V.wait_ge(s_sP.h, s_sP.v)
            if ci > 0:
                V.wait_ge(sst.h, u_dma_done)
                V.wait_ge(s_t.h, t_xproj_done)
            for j in range(16):
                V.tensor_scalar(out=uf_v[:, j, :], in0=xin_v[:, j, 1:1 + TCP],
                                scalar1=wconv_sb[:, 4 * j:4 * j + 1], scalar2=None,
                                op0=ALU.mult)
                for k in (1, 2, 3):
                    V.scalar_tensor_tensor(out=uf_v[:, j, :],
                                           in0=xin_v[:, j, 1 + k:1 + k + TCP],
                                           scalar=wconv_sb[:, 4 * j + k:4 * j + k + 1],
                                           in1=uf_v[:, j, :], op0=ALU.mult, op1=ALU.add)
                V.tensor_copy(xin_v[:, j, 0:4], xin_v[:, j, TCP:TCP + 4])
            ins = bnop(s_v, V)
            g_conv_done = s_v.v
            SC.wait_ge(s_v.h, g_conv_done)
            for j in range(16):
                SC.activation(out=uf_v[:, j, :], in_=uf_v[:, j, :], func=AF.Silu,
                              bias=convb_sb[:, j:j + 1])
            bnop(s_s, SC)
            V.wait_ge(s_s.h, s_s.v)
            V.tensor_copy(ub[:], uf[:])
            bnop(s_v, V)
            SY.wait_ge(s_v.h, s_v.v)
            dma = bump(sst, SY.dma_start(out=usp_v[:, :, t0:t0 + TCP], in_=uf_v), 16)
            u_dma_done = sst.v

            T.wait_ge(s_v.h, s_v.v)
            if ci > 0:
                T.wait_ge(sst.h, xdbl_dma_done)
            for g in range(16):
                ins = T.matmul(ps_e[0:33, 0:TCP], w_xp_v[:, g, :], ub_v[:, g, :],
                               start=(g == 0), stop=(g == 15))
            bump(s_t, ins)
            t_xproj_done = s_t.v
            SC.wait_ge(s_t.h, s_t.v)
            SC.copy(out=xdblf[:], in_=ps_e[0:33, 0:TCP])
            bnop(s_s, SC)
            SY.wait_ge(s_s.h, s_s.v)
            bump(sst, SY.dma_start(out=xdbl_sp[:, t0:t0 + TCP], in_=xdblf[:]), 16)
            xdbl_dma_done = sst.v

            T.wait_ge(s_s.h, s_s.v)
            if ci > 0:
                T.wait_ge(s_v.h, v_dt_done)
            ins = T.matmul(ps_f[:, 0:TCP], ones_row[:], xdblf[0:1, :], start=True,
                           stop=True)
            bump(s_t, ins)
            t_dtbc_done = s_t.v
            V.wait_ge(s_t.h, t_dtbc_done)
            if ci > 0:
                V.wait_ge(sst.h, dtdu_dma_done)
            for j in range(16):
                V.tensor_scalar(out=dtf_v[:, j, :], in0=ps_f[:, 0:TCP],
                                scalar1=wdtw_sb[:, j:j + 1], scalar2=dtb_sb[:, j:j + 1],
                                op0=ALU.mult, op1=ALU.add)
            ins = bnop(s_v, V)
            v_dt_done = s_v.v
            SC.wait_ge(s_v.h, v_dt_done)
            for j in range(16):
                SC.activation(out=spa_v[:, j, :], in_=dtf_v[:, j, :], func=AF.Abs)
            for j in range(16):
                SC.activation(out=spa_v[:, j, :], in_=spa_v[:, j, :], func=AF.Exp,
                              scale=-1.0)
            for j in range(16):
                SC.activation(out=spa_v[:, j, :], in_=spa_v[:, j, :], func=AF.Ln,
                              bias=1.0)
            bnop(s_s, SC)
            V.wait_ge(s_s.h, s_s.v)
            for j in range(16):
                V.tensor_scalar(out=dtf_v[:, j, :], in0=dtf_v[:, j, :], scalar1=0.0,
                                scalar2=None, op0=ALU.max)
                V.tensor_tensor(out=dtf_v[:, j, :], in0=dtf_v[:, j, :],
                                in1=spa_v[:, j, :], op=ALU.add)
                V.tensor_scalar(out=dtf_v[:, j, :], in0=dtf_v[:, j, :], scalar1=10.0,
                                scalar2=1e-4, op0=ALU.min, op1=ALU.max)
                V.tensor_tensor(out=duf_v[:, j, :], in0=dtf_v[:, j, :],
                                in1=uf_v[:, j, :], op=ALU.mult)
            V.tensor_copy(hib[:], dtf[:])
            V.tensor_copy(hi32[:], hib[:])
            V.tensor_tensor(out=lob[:], in0=dtf[:], in1=hi32[:], op=ALU.subtract)
            bnop(s_v, V)
            SY.wait_ge(s_v.h, s_v.v)
            bump(sst, SY.dma_start(out=dthi_v[:, :, t0:t0 + TCP], in_=hib_v), 16)
            bump(sst, SY.dma_start(out=dtlo_v[:, :, t0:t0 + TCP], in_=lob_v), 16)
            hb_dma = sst.v
            V.wait_ge(sst.h, hb_dma)
            V.tensor_copy(hib[:], duf[:])
            V.tensor_copy(hi32[:], hib[:])
            V.tensor_tensor(out=lob[:], in0=duf[:], in1=hi32[:], op=ALU.subtract)
            bnop(s_v, V)
            SY.wait_ge(s_v.h, s_v.v)
            bump(sst, SY.dma_start(out=duhi_v[:, :, t0:t0 + TCP], in_=hib_v), 16)
            bump(sst, SY.dma_start(out=dulo_v[:, :, t0:t0 + TCP], in_=lob_v), 16)
            dtdu_dma_done = sst.v
            v_dtdu_done = s_v.v

        SY.wait_ge(sst.h, sst.v)
        phase_barrier()
        sb_free(pre_mark)
        scan_mark = sb_scope()
        Bcols = sb("Bcols", [128, M], F32)
        Ccols = sb("Ccols", [128, M], F32)
        dt8 = sb("dt8", [8, 2 * 2 * 256 * TC], BF16)
        du8 = sb("du8", [8, 2 * 2 * 256 * TC], BF16)
        dA_b = sb("dA_b", [128, 2 * 256 * TC], F32)
        dtu_b = sb("dtu_b", [128, 2 * 256 * TC], F32)
        hh = sb("hh", [128, 2 * TC * 256], F32)
        hzero = sb("hzero", [128, 256], F32)
        hda = sb("hda", [128, 256], F32)
        hpre = sb("hpre", [128, 256], F32)
        stat_b = sb("stat_b", [128, 2 * TC * 8], F32)
        yev = sb("yev", [128, 2 * 512], F32)
        dtc = sb("dtc", [128, 256], F32)
        V.memset(ps_g[:], 0.0)
        V.memset(ps_h[:], 0.0)
        V.memset(hzero[:], 0.0)
        V.memset(hh[:], 0.0)
        bnop(s_v, V)
        G.wait_ge(s_v.h, s_v.v)
        bump(sld, SY.dma_start(
            out=Bcols[:],
            in_=xdbl_sp[1:1 + DS, :].unsqueeze(1).broadcast_to([DS, 8, M])), 16)
        bump(sld, SY.dma_start(
            out=Ccols[:],
            in_=xdbl_sp[1 + DS:1 + 2 * DS, :].unsqueeze(1).broadcast_to([DS, 8, M])), 16)
        bc_loaded = sld.v

        dthi8 = dthi_sp.rearrange("(b d) t -> b d t", b=8)
        dtlo8 = dtlo_sp.rearrange("(b d) t -> b d t", b=8)
        duhi8 = duhi_sp.rearrange("(b d) t -> b d t", b=8)
        dulo8 = dulo_sp.rearrange("(b d) t -> b d t", b=8)
        dt8_v = dt8[:].rearrange("b (u h d t) -> b u h d t", u=2, h=2, d=256)
        du8_v = du8[:].rearrange("b (u h d t) -> b u h d t", u=2, h=2, d=256)
        dA_v = dA_b[:].rearrange("p (u d t) -> p u d t", u=2, d=256)
        dtu_v = dtu_b[:].rearrange("p (u d t) -> p u d t", u=2, d=256)
        hh_v = hh[:].rearrange("p (u t d) -> p u t d", u=2, t=TC)
        stat_v = stat_b[:].rearrange("p (u t e) -> p u t e", u=2, t=TC)
        yev_v = yev[:].rearrange("p (u f) -> p u f", u=2)
        ysp_v = y_sp.rearrange("(g c j) (b d) -> g j b c d", c=2, j=4, b=8)
        dpsum = [ps_a, ps_b]
        upsum = [ps_c, ps_d]
        ypsum = [ps_g, ps_h]

        ydma_vals = {}
        ld_vals = {}

        def emit_loads(k):
            pb = k % 2
            t0 = k * TC
            SY.wait_ge(sld.h, sld.v)
            bump(sld, SY.dma_start(out=dt8_v[:, pb, 0, :, :], in_=dthi8[:, :, t0:t0 + TC]), 16)
            bump(sld, SY.dma_start(out=dt8_v[:, pb, 1, :, :], in_=dtlo8[:, :, t0:t0 + TC]), 16)
            bump(sld, SY.dma_start(out=du8_v[:, pb, 0, :, :], in_=duhi8[:, :, t0:t0 + TC]), 16)
            bump(sld, SY.dma_start(out=du8_v[:, pb, 1, :, :], in_=dulo8[:, :, t0:t0 + TC]), 16)
            ld_vals[k] = sld.v

        def emit_repl(k):
            pb = k % 2
            T.wait_ge(sld.h, ld_vals[k])
            if k >= 2:
                T.wait_ge(s_sA.h, 4 * (k - 1))
                T.wait_ge(s_sU.h, 4 * (k - 1))
            for q in range(4):
                qb = q % 2
                rh = dt8_v[:, pb, 0, 64 * q:64 * q + 64, :]
                rl = dt8_v[:, pb, 1, 64 * q:64 * q + 64, :]
                if q >= 2:
                    T.wait_ge(s_sA.h, 4 * k + q - 1)
                T.matmul(dpsum[qb][:, 0:512], mask8_sb[:], rh, start=True, stop=False)
                ins = T.matmul(dpsum[qb][:, 0:512], mask8_sb[:], rl, start=False, stop=True)
                bump(s_tA, ins)
                rh = du8_v[:, pb, 0, 64 * q:64 * q + 64, :]
                rl = du8_v[:, pb, 1, 64 * q:64 * q + 64, :]
                if q >= 2:
                    T.wait_ge(s_sU.h, 4 * k + q - 1)
                T.matmul(upsum[qb][:, 0:512], mask8_sb[:], rh, start=True, stop=False)
                ins = T.matmul(upsum[qb][:, 0:512], mask8_sb[:], rl, start=False, stop=True)
                bump(s_tU, ins)
            if k >= 2:
                SC.wait_ge(s_y.h, TC * (k - 1))
            for q in range(4):
                qb = q % 2
                SC.wait_ge(s_tA.h, 4 * k + q + 1)
                ins = SC.activation(out=dA_v[:, pb, 64 * q:64 * q + 64, :],
                                    in_=dpsum[qb][:, 0:512], func=AF.Exp, scale=A_col[:])
                bump(s_sA, ins)
                SC.wait_ge(s_tU.h, 4 * k + q + 1)
                ins = SC.copy(out=dtu_v[:, pb, 64 * q:64 * q + 64, :],
                              in_=upsum[qb][:, 0:512])
                bump(s_sU, ins)

        def emit_stat(k):
            pb = k % 2
            t0 = k * TC
            if k < 3:
                V.wait_ge(sld.h, bc_loaded)
            if k >= 2:
                V.wait_ge(s_ye.h, k - 1)
            V.tensor_tensor(out=stat_v[:, pb, :, :],
                            in0=Ccols[:, t0:t0 + TC].unsqueeze(2).broadcast_to([128, TC, 8]),
                            in1=maskf_sb[:].unsqueeze(1).broadcast_to([128, TC, 8]),
                            op=ALU.mult)

        def emit_vloop(k):
            pb = k % 2
            t0 = k * TC
            V.wait_ge(s_sA.h, 4 * (k + 1))
            V.wait_ge(s_sU.h, 4 * (k + 1))
            if k >= 2:
                V.wait_ge(s_ye.h, k - 1)
            for w in range(TC):
                hprev = hzero[:] if (k == 0 and w == 0) else (
                    hh_v[:, (k - 1) % 2, TC - 1, :] if w == 0 else hh_v[:, pb, w - 1, :])
                V.scalar_tensor_tensor(out=hda[:], in0=dA_v[:, pb, :, w],
                                       scalar=EXP_FLOOR, in1=hprev,
                                       op0=ALU.max, op1=ALU.mult)
                V.tensor_scalar(out=dtc[:], in0=dtu_v[:, pb, :, w],
                                scalar1=Bcols[:, t0 + w:t0 + w + 1], scalar2=10.0,
                                op0=ALU.mult, op1=ALU.min)
                V.scalar_tensor_tensor(out=hpre[:], in0=dtc[:],
                                       scalar=-10.0, in1=hda[:],
                                       op0=ALU.max, op1=ALU.add)
                ins = V.tensor_scalar(out=hh_v[:, pb, w, :], in0=hpre[:], scalar1=100.0,
                                      scalar2=-100.0, op0=ALU.min, op1=ALU.max)
                bump(s_y, ins)

        def emit_y(k):
            pb = k % 2
            if k >= 2:
                T.wait_ge(s_yev.h, k - 1)
            for w in range(TC):
                j = w % 4
                c = w // 4
                T.wait_ge(s_y.h, TC * k + w + 1)
                ins = T.matmul(ypsum[pb][32 * j:32 * j + 8, 256 * c:256 * c + 256],
                               stat_v[:, pb, w, :], hh_v[:, pb, w, :],
                               start=True, stop=True, tile_position=(0, 32 * j))
            bump(s_ye, ins)

        def emit_yevac(k):
            pb = k % 2
            SC.wait_ge(s_ye.h, k + 1)
            if k >= 2:
                SC.wait_ge(sst.h, ydma_vals[k - 2])
            ins = SC.copy(out=yev_v[:, pb, :], in_=ypsum[pb][:, 0:512])
            bump(s_yev, ins)
            SY.wait_ge(s_yev.h, k + 1)
            SY.wait_ge(sst.h, sst.v)
            for j in range(4):
                bump(sst, SY.dma_start(
                    out=ysp_v[k, j],
                    in_=yev_v[:, pb, :].rearrange("p (c d) -> p c d", c=2)[32 * j:32 * j + 8, :, :]), 16)
            ydma_vals[k] = sst.v

        SC.wait_ge(sld.h, bc_loaded)
        emit_loads(0)
        emit_loads(1)
        emit_loads(2)
        emit_loads(3)
        emit_repl(0)
        emit_repl(1)
        emit_stat(0)
        for k in range(NSC):
            if k + 4 < NSC:
                SY.wait_ge(s_tU.h, 4 * (k + 3))
                emit_loads(k + 4)
            if k + 2 < NSC:
                emit_repl(k + 2)
            if k + 1 < NSC:
                emit_stat(k + 1)
            emit_vloop(k)
            emit_y(k)
            emit_yevac(k)

        SY.wait_ge(sst.h, sst.v)
        phase_barrier()
        sb_free(scan_mark)
        w_out_sb = sb("w_out_sb", [128, 16 * DM], BF16)
        phi_sb = sb("phi_sb", [128, 8 * NP], BF16)
        ytm = sb("ytm", [128, 2 * DI], F32)
        ynn = sb("ynn", [128, 2 * DI], F32)
        mcol = sb("mcol", [128, 2], F32)
        vcol = sb("vcol", [128, 2], F32)
        rcol = sb("rcol", [128, 2], F32)
        ynT = sb("ynT", [128, 16 * TPO], F32)
        uT_po = sb("uT_po", [128, 16 * TPO], F32)
        zT_po = sb("zT_po", [128, 16 * TPO], F32)
        szT = sb("szT", [128, 16 * TPO], F32)
        gatT = sb("gatT", [128, 16 * TPO], BF16)
        rt_po = sb("rt_po", [128, 8 * TPO], F32)
        r1b_t = sb("r1b_t", [128, 8 * TPO], BF16)
        r1sq_t = sb("r1sq_t", [128, 8 * TPO], BF16)
        x1lnb = sb("x1lnb", [128, 8 * TPO], BF16)
        t_sx = sb("t_sx", [1, TPO], F32)
        t_mu = sb("t_mu", [1, TPO], F32)
        t_var = sb("t_var", [1, TPO], F32)
        t_rstd = sb("t_rstd", [1, TPO], F32)
        t_mur = sb("t_mur", [1, TPO], F32)
        elb = sb("elb", [128, 2 * NP], BF16)
        rs4 = sb("rs4", [128, 4], F32)
        rsc = sb("rsc", [128, 2], F32)
        lnrsf = sb("lnrsf", [128, 2], F32)
        lnrsb = sb("lnrsb", [128, 2], BF16)
        lrhi32 = sb("lrhi32", [128, 2], F32)
        lrlo_b = sb("lrlo_b", [128, 2], BF16)
        negrow_b = sb("negrow_b", [1, 2 * TPO], BF16)
        eT_b = sb("eT_b", [128, 2 * TPO], BF16)
        bump(sld, SY.dma_start(out=w_out_sb[:].rearrange("p (g c) -> p g c", g=16),
                               in_=w_outproj.rearrange("(g p) c -> p g c", p=128)), 16)
        bump(sld, SY.dma_start(out=phi_sb[:].rearrange("p (g c) -> p g c", g=8),
                               in_=phi.rearrange("(g p) c -> p g c", p=128)), 16)
        post_w_loaded = sld.v
        T.wait_ge(sld.h, post_w_loaded)
        SC.wait_ge(sld.h, post_w_loaded)
        xT_v = xT.rearrange("(g p) t -> p g t", p=128)
        r1_v = r1_out.rearrange("(g p) t -> p g t", p=128)
        phi_v = phi_sb[:].rearrange("p (g c) -> p g c", g=8)
        w_out_v = w_out_sb[:].rearrange("p (g c) -> p g c", g=16)
        ytm_v = ytm[:].rearrange("p (s d) -> p s d", s=2)
        ynn_v = ynn[:].rearrange("p (s d) -> p s d", s=2)
        ynT_v = ynT[:].rearrange("p (j t) -> p j t", j=16)
        uT_v = uT_po[:].rearrange("p (j t) -> p j t", j=16)
        zT_v = zT_po[:].rearrange("p (j t) -> p j t", j=16)
        szT_v = szT[:].rearrange("p (j t) -> p j t", j=16)
        gatT_v = gatT[:].rearrange("p (j t) -> p j t", j=16)
        rt_v = rt_po[:].rearrange("p (g t) -> p g t", g=8)
        r1b_v = r1b_t[:].rearrange("p (g t) -> p g t", g=8)
        r1sq_v = r1sq_t[:].rearrange("p (g t) -> p g t", g=8)
        x1_v = x1lnb[:].rearrange("p (g t) -> p g t", g=8)
        eT_dma = {0: 0, 1: 0}
        x1tok_dma = 0

        for ci in range(NPO):
            t0 = ci * TPO
            SY.wait_ge(sld.h, sld.v)
            SY.wait_ge(sst.h, sst.v)
            if ci > 0:
                SY.wait_ge(s_t.h, s_t.v)
            bump(sld, SY.dma_start(out=ytm_v,
                                   in_=y_sp[t0:t0 + TPO, :].rearrange("(s p) d -> p s d", p=128)), 16)
            bump(sld, SY.dma_start(out=uT_v, in_=usp_v[:, :, t0:t0 + TPO]), 16)
            bump(sld, SY.dma_start(out=zT_v, in_=zsp_v[:, :, t0:t0 + TPO]), 16)
            bump(sld, SY.dma_start(out=rt_v, in_=xT_v[:, :, t0:t0 + TPO]), 16)
            V.wait_ge(sld.h, sld.v)
            for sdx in range(2):
                V.tensor_reduce(out=mcol[:, sdx:sdx + 1], in_=ytm_v[:, sdx, :],
                                axis=AX.X, op=ALU.add)
            bnop(s_v, V)
            SC.wait_ge(s_v.h, s_v.v)
            for sdx in range(2):
                SC.activation(out=ynn_v[:, sdx, :], in_=ytm_v[:, sdx, :], func=AF.Square)
            bnop(s_s, SC)
            V.wait_ge(s_s.h, s_s.v)
            for sdx in range(2):
                V.tensor_reduce(out=vcol[:, sdx:sdx + 1], in_=ynn_v[:, sdx, :],
                                axis=AX.X, op=ALU.add)
            bnop(s_v, V)
            V.tensor_scalar(out=mcol[:], in0=mcol[:], scalar1=1.0 / DI, scalar2=None,
                            op0=ALU.mult)
            V.tensor_scalar(out=vcol[:], in0=vcol[:], scalar1=1.0 / DI, scalar2=None,
                            op0=ALU.mult)
            V.drain()
            V.tensor_tensor(out=rcol[:], in0=mcol[:], in1=mcol[:], op=ALU.mult)
            V.drain()
            V.tensor_tensor(out=vcol[:], in0=vcol[:], in1=rcol[:], op=ALU.subtract)
            bnop(s_v, V)
            SC.wait_ge(s_v.h, s_v.v)
            SC.activation(out=vcol[:], in_=vcol[:], func=AF.Relu)
            SC.activation(out=vcol[:], in_=vcol[:], func=AF.Ln, bias=LN_EPS)
            SC.activation(out=rcol[:], in_=vcol[:], func=AF.Exp, scale=-0.5)
            bnop(s_s, SC)
            V.wait_ge(s_s.h, s_s.v)
            for sdx in range(2):
                V.tensor_scalar(out=ynn_v[:, sdx, :], in0=ytm_v[:, sdx, :],
                                scalar1=mcol[:, sdx:sdx + 1], scalar2=rcol[:, sdx:sdx + 1],
                                op0=ALU.subtract, op1=ALU.mult)
            bnop(s_v, V)
            SY.wait_ge(s_v.h, s_v.v)
            bump(sst, SY.dma_start(out=yn_sp[t0:t0 + TPO, :].rearrange("(s p) d -> p s d", p=128),
                                   in_=ynn_v), 16)
            T.wait_ge(s_v.h, s_v.v)
            for j in range(16):
                for sdx in range(2):
                    pt = ps_e if (j * 2 + sdx) % 2 == 0 else ps_f
                    if ci > 0 or j * 2 + sdx >= 2:
                        T.wait_ge(s_s.h, s_s.v)
                    ins = T.transpose(pt[:, 0:128], ynn_v[:, sdx, 128 * j:128 * j + 128],
                                      identf_sb[:])
                    bump(s_t, ins)
                    SC.wait_ge(s_t.h, s_t.v)
                    ins = SC.copy(out=ynT_v[:, j, 128 * sdx:128 * sdx + 128],
                                  in_=pt[:, 0:128])
                    bump(s_s, ins)
            SC.wait_ge(sld.h, sld.v)
            for j in range(16):
                SC.activation(out=szT_v[:, j, :], in_=zT_v[:, j, :], func=AF.Silu)
            bnop(s_s, SC)
            V.wait_ge(s_s.h, s_s.v)
            for j in range(16):
                V.tensor_scalar(out=ynT_v[:, j, :], in0=ynT_v[:, j, :],
                                scalar1=lng_sb[:, j:j + 1], scalar2=lnb_sb[:, j:j + 1],
                                op0=ALU.mult, op1=ALU.add)
                V.scalar_tensor_tensor(out=ynT_v[:, j, :], in0=uT_v[:, j, :],
                                       scalar=dvec_sb[:, j:j + 1], in1=ynT_v[:, j, :],
                                       op0=ALU.mult, op1=ALU.add)
                V.tensor_tensor(out=gatT_v[:, j, :], in0=ynT_v[:, j, :],
                                in1=szT_v[:, j, :], op=ALU.mult)
            bnop(s_v, V)
            SY.wait_ge(s_v.h, s_v.v)
            SY.wait_ge(sst.h, sst.v)
            bump(sst, SY.dma_start(out=gat_sp.rearrange("(j p) t -> p j t", p=128)[:, :, t0:t0 + TPO],
                                   in_=gatT_v), 16)
            T.wait_ge(s_v.h, s_v.v)
            for it in range(8):
                pt = ps_e if it % 2 == 0 else ps_f
                if it >= 2:
                    T.wait_ge(s_v.h, s_v.v)
                for g in range(16):
                    ins = T.matmul(pt[:, 0:TPO], w_out_v[:, g, 128 * it:128 * it + 128],
                                   gatT_v[:, g, :], start=(g == 0), stop=(g == 15))
                bump(s_t, ins)
                V.wait_ge(s_t.h, s_t.v)
                ins = V.tensor_tensor(out=rt_v[:, it, :], in0=rt_v[:, it, :],
                                      in1=pt[:, 0:TPO], op=ALU.add)
                bump(s_v, ins)
            SY.wait_ge(s_v.h, s_v.v)
            bump(sst, SY.dma_start(out=r1_v[:, :, t0:t0 + TPO], in_=rt_v), 16)

            V.tensor_copy(r1b_t[:], rt_po[:])
            bnop(s_v, V)
            SC.wait_ge(s_v.h, s_v.v)
            if ci > 0:
                SC.wait_ge(sst.h, x1tok_dma)
            SC.activation(out=r1sq_t[:], in_=rt_po[:], func=AF.Square)
            bnop(s_s, SC)
            T.wait_ge(s_s.h, s_s.v)
            for g in range(8):
                T.matmul(ps_e[0:1, 0:TPO], onesc_bf[:], r1b_v[:, g, :],
                         start=(g == 0), stop=(g == 7))
            for g in range(8):
                ins = T.matmul(ps_f[0:1, 0:TPO], onesc_bf[:], r1sq_v[:, g, :],
                               start=(g == 0), stop=(g == 7))
            bump(s_t, ins)
            V.wait_ge(s_t.h, s_t.v)
            V.tensor_scalar(out=t_mu[:], in0=ps_e[0:1, 0:TPO], scalar1=1.0 / DM,
                            scalar2=None, op0=ALU.mult)
            V.tensor_scalar(out=t_var[:], in0=ps_f[0:1, 0:TPO], scalar1=1.0 / DM,
                            scalar2=None, op0=ALU.mult)
            V.tensor_tensor(out=t_sx[:], in0=t_mu[:], in1=t_mu[:], op=ALU.mult)
            V.tensor_tensor(out=t_var[:], in0=t_var[:], in1=t_sx[:], op=ALU.subtract)
            bnop(s_v, V)
            SC.wait_ge(s_v.h, s_v.v)
            SC.activation(out=t_var[:], in_=t_var[:], func=AF.Relu)
            SC.activation(out=t_var[:], in_=t_var[:], func=AF.Ln, bias=LN_EPS)
            SC.activation(out=t_rstd[:], in_=t_var[:], func=AF.Exp, scale=-0.5)
            bnop(s_s, SC)
            V.wait_ge(s_s.h, s_s.v)
            V.tensor_tensor(out=t_mur[:], in0=t_mu[:], in1=t_rstd[:], op=ALU.mult)
            bnop(s_v, V)
            T.wait_ge(s_v.h, s_v.v)
            T.matmul(ps_e[:, 0:TPO], ones_row[:], t_rstd[:], start=True, stop=True)
            ins = T.matmul(ps_f[:, 0:TPO], ones_row[:], t_mur[:], start=True, stop=True)
            bump(s_t, ins)
            V.wait_ge(s_t.h, s_t.v)
            x1a_v = ynT[:, 0:8 * TPO].rearrange("p (g t) -> p g t", g=8)
            for g in range(8):
                V.tensor_tensor(out=x1a_v[:, g, :], in0=rt_v[:, g, :], in1=ps_e[:, 0:TPO],
                                op=ALU.mult)
                V.tensor_tensor(out=x1_v[:, g, :], in0=x1a_v[:, g, :],
                                in1=ps_f[:, 0:TPO], op=ALU.subtract)
            bnop(s_v, V)
            T.wait_ge(s_v.h, s_v.v)
            for it in range(TPO // 128):
                for nsp in range(4):
                    pt = ps_e if (it * 4 + nsp) % 2 == 0 else ps_f
                    T.wait_ge(s_s.h, s_s.v)
                    for g in range(8):
                        ins = T.matmul(pt[:, 0:512], x1_v[:, g, 128 * it:128 * it + 128],
                                       phi_v[:, g, 512 * nsp:512 * nsp + 512],
                                       start=(g == 0), stop=(g == 7))
                    bump(s_t, ins)
                    SC.wait_ge(s_t.h, s_t.v)
                    eb = elb[:, (it % 2) * NP:(it % 2) * NP + NP]
                    ins = SC.activation(out=eb[:, 512 * nsp:512 * nsp + 512],
                                        in_=pt[:, 0:512], func=AF.Exp)
                    bump(s_s, ins)
                SY.wait_ge(s_s.h, s_s.v)
                bump(sst, SY.dma_start(out=el1_out[t0 + 128 * it:t0 + 128 * it + 128, :],
                                       in_=elb[:, (it % 2) * NP:(it % 2) * NP + NP]), 16)

            x1tok_v = r1sq_t[:].rearrange("p (u d) -> p u d", u=2)
            if _EXPORT_X1TOK:
                if ci > 0:
                    SC.wait_ge(sst.h, x1tok_dma)
                for u in range(2):
                    for g in range(8):
                        pt = ps_e if g % 2 == 0 else ps_f
                        T.wait_ge(s_s.h, s_s.v)
                        ins = T.matmul(pt[:, 0:128], x1_v[:, g, 128 * u:128 * u + 128],
                                       identb_sb[:], start=True, stop=True)
                        bump(s_t, ins)
                        SC.wait_ge(s_t.h, s_t.v)
                        ins = SC.copy(out=x1tok_v[:, u, 128 * g:128 * g + 128],
                                      in_=pt[:, 0:128])
                        bump(s_s, ins)
                    SY.wait_ge(s_s.h, s_s.v)
                    SY.wait_ge(sst.h, sst.v)
                    bump(sst, SY.dma_start(out=x1lntok_out[t0 + 128 * u:t0 + 128 * u + 128, :],
                                           in_=x1tok_v[:, u, :]), 16)
                x1tok_dma = sst.v

            V.wait_ge(s_s.h, s_s.v)
            for it in range(2):
                eb = elb[:, it * NP:it * NP + NP]
                for q in range(4):
                    V.tensor_reduce(out=rs4[:, q:q + 1], in_=eb[:, 512 * q:512 * q + 512],
                                    axis=AX.X, op=ALU.add)
                V.drain()
                V.tensor_reduce(out=rsc[:, it:it + 1], in_=rs4[:], axis=AX.X, op=ALU.add)
                V.drain()
            bnop(s_v, V)
            SC.wait_ge(s_v.h, s_v.v)
            SC.activation(out=lnrsf[:], in_=rsc[:], func=AF.Ln)
            bnop(s_s, SC)
            V.wait_ge(s_s.h, s_s.v)
            V.tensor_copy(lnrsb[:], lnrsf[:])
            V.drain()
            V.tensor_copy(lrhi32[:], lnrsb[:])
            V.drain()
            V.tensor_tensor(out=lrlo_b[:], in0=lnrsf[:], in1=lrhi32[:], op=ALU.subtract)
            bnop(s_v, V)
            for hl, src in ((0, lnrsb), (1, lrlo_b)):
                for it in range(2):
                    T.wait_ge(s_v.h, s_v.v)
                    T.wait_ge(s_s.h, s_s.v)
                    ins = T.matmul(ps_e[0:1, 0:128], src[:, it:it + 1], identb_sb[:],
                                   start=True, stop=True)
                    bump(s_t, ins)
                    SC.wait_ge(s_t.h, s_t.v)
                    ins = SC.activation(
                        out=negrow_b[0:1, TPO * hl + 128 * it:TPO * hl + 128 * it + 128],
                        in_=ps_e[0:1, 0:128], func=AF.Copy, scale=-1.0)
                    bump(s_s, ins)

            for npb in range(16 if _EXPORT_ELT else 0):
                pt = ps_e if npb % 2 == 0 else ps_f
                T.wait_ge(s_s.h, s_s.v)
                for g in range(8):
                    T.matmul(pt[:, 0:TPO], phi_v[:, g, 128 * npb:128 * npb + 128],
                             x1_v[:, g, :], start=(g == 0), stop=False)
                T.matmul(pt[:, 0:TPO], onesr_b[:], negrow_b[0:1, 0:TPO],
                         start=False, stop=False)
                ins = T.matmul(pt[:, 0:TPO], onesr_b[:], negrow_b[0:1, TPO:2 * TPO],
                               start=False, stop=True)
                bump(s_t, ins)
                SC.wait_ge(s_t.h, s_t.v)
                if ci > 0 or npb >= 2:
                    SC.wait_ge(sst.h, eT_dma[npb % 2])
                ins = SC.activation(out=eT_b[:, (npb % 2) * TPO:(npb % 2) * TPO + TPO],
                                    in_=pt[:, 0:TPO], func=AF.Exp)
                bump(s_s, ins)
                SY.wait_ge(s_s.h, s_s.v)
                SY.wait_ge(sst.h, sst.v)
                bump(sst, SY.dma_start(
                    out=el1Tsc_out[128 * npb:128 * npb + 128, t0:t0 + TPO],
                    in_=eT_b[:, (npb % 2) * TPO:(npb % 2) * TPO + TPO]), 16)
                eT_dma[npb % 2] = sst.v

        for name, (out_ap, src_ap) in dbg_outs.items():
            SY.wait_ge(sst.h, sst.v)
            bump(sst, SY.dma_start(out=out_ap, in_=src_ap), 16)

        SY.wait_ge(sst.h, sst.v)

    for cm in reversed(ctx):
        cm.__exit__(None, None, None)
    return nc



def build_L2():
    nc = bass.Bass("TRN2", target_bir_lowering=False, detect_race_conditions=False)

    r1T = nc.dram_tensor("r1T", [DM, M], F32, kind="ExternalInput").ap()
    x1lntok = nc.dram_tensor("x1lntok", [M, DM], BF16, kind="ExternalInput").ap()
    el_in = [nc.dram_tensor(f"el_in{b}", [M, NP], BF16, kind="ExternalInput").ap()
             for b in range(8)]
    elTsc = nc.dram_tensor("elTsc", [NP, M], BF16, kind="ExternalInput").ap()
    wffn = nc.dram_tensor("wffn", [NEXP, DM, DM], BF16, kind="ExternalInput").ap()
    bffn_row = nc.dram_tensor("bffn_row", [1, NEXP * DM], BF16, kind="ExternalInput").ap()
    wqkv = nc.dram_tensor("wqkv", [DM, 3 * DM], BF16, kind="ExternalInput").ap()
    bqkv_col = nc.dram_tensor("bqkv_col", [3 * DM, 1], F32, kind="ExternalInput").ap()
    bv_row = nc.dram_tensor("bv_row", [1, DM], BF16, kind="ExternalInput").ap()
    wout_in = nc.dram_tensor("wout", [DM, DM], BF16, kind="ExternalInput").ap()
    bout_row = nc.dram_tensor("bout_row", [1, DM], BF16, kind="ExternalInput").ap()
    phi = nc.dram_tensor("phi", [DM, NP], BF16, kind="ExternalInput").ap()
    diagm_in = nc.dram_tensor("diagm", [128, 4 * 512], BF16, kind="ExternalInput").ap()
    identb = nc.dram_tensor("identb", [128, 128], BF16, kind="ExternalInput").ap()
    onesrb_in = nc.dram_tensor("onesrb", [1, 128], BF16, kind="ExternalInput").ap()
    ones512b_in = nc.dram_tensor("ones512b", [1, 512], BF16, kind="ExternalInput").ap()
    onesrf_in = nc.dram_tensor("onesrf", [1, 128], F32, kind="ExternalInput").ap()
    onescb_in = nc.dram_tensor("onescb", [128, 1], BF16, kind="ExternalInput").ap()

    x3T_out = nc.dram_tensor("x3T", [DM, M], F32, kind="ExternalOutput").ap()
    x3lnT_out = nc.dram_tensor("x3lnT", [DM, M], BF16, kind="ExternalOutput").ap()
    el2_out = nc.dram_tensor("el2", [M, NP], BF16, kind="ExternalOutput").ap()
    el2Tsc_out = nc.dram_tensor("el2Tsc", [NP, M], BF16, kind="ExternalOutput").ap()

    x2T_sp = nc.dram_tensor("x2T_sp", [DM, M], F32, kind="Internal").ap()
    x2lnT_sp = nc.dram_tensor("x2lnT_sp", [DM, M], BF16, kind="Internal").ap()
    qT_sp = nc.dram_tensor("qT_sp", [DM, M], BF16, kind="Internal").ap()
    ctxT_sp = nc.dram_tensor("ctxT_sp", [DM, M], BF16, kind="Internal").ap()
    y1_sp = nc.dram_tensor("y1_sp", [NP, DM], BF16, kind="Internal").ap()
    g1_sp = nc.dram_tensor("g1_sp", [M, NP], BF16, kind="Internal").ap()
    slots_sp = nc.dram_tensor("slots_sp", [DM, NP], BF16, kind="Internal").ap()
    k_sp = nc.dram_tensor("k_sp", [DM, M], BF16, kind="Internal").ap()
    vext_sp = nc.dram_tensor("vext_sp", [M, 1040], BF16, kind="Internal").ap()

    NMC = M // 128
    NM4 = M // 512
    TCH = 512
    NTC = M // TCH

    ctx = []

    def sb(name, shape, dt):
        cm = nc.sbuf_tensor(name, shape, dt)
        t = cm.__enter__()
        ctx.append(cm)
        return t

    def psum(name, shape=(128, 512), dt=F32):
        cm = nc.psum_tensor(name, list(shape), dt)
        t = cm.__enter__()
        ctx.append(cm)
        return t

    def sb_scope():
        return len(ctx)

    def sb_free(mark):
        while len(ctx) > mark:
            ctx.pop().__exit__(None, None, None)

    pA = psum("pA")
    pB = psum("pB")
    pC = psum("pC")
    pD = psum("pD")
    pE = psum("pE")
    pF = psum("pF")
    pG = psum("pG")
    pH = psum("pH")

    sld = S(nc, "sld")
    sst = S(nc, "sst")
    s_t = S(nc, "s_t")
    s_v = S(nc, "s_v")
    s_s = S(nc, "s_s")
    s_ph = S(nc, "s_ph")

    SY = nc.sync
    V = nc.vector
    SC = nc.scalar
    T = nc.tensor
    G = nc.gpsimd

    def phase_barrier():
        for eng in (SY, V, SC, T, G):
            bnop(s_ph, eng)
        for eng in (SY, V, SC, T, G):
            eng.wait_ge(s_ph.h, s_ph.v)

    onesr_b = sb("onesr_b", [1, 128], BF16)
    ones512_b = sb("ones512_b", [1, 512], BF16)
    onesr_f = sb("onesr_f", [1, 128], F32)
    onesc_b = sb("onesc_b", [128, 1], BF16)
    identb_sb = sb("identb_sb", [128, 128], BF16)
    cst_sb = sb("cst_sb", [128, 8], F32)

    with nc.Block() as _block:
        for dst, src in [
            (onesr_b[:], onesrb_in), (ones512_b[:], ones512b_in),
            (onesr_f[:], onesrf_in), (onesc_b[:], onescb_in),
            (identb_sb[:], identb),
        ]:
            bump(sld, SY.dma_start(out=dst, in_=src), 16)
        for ii, cv in enumerate([LN_EPS, 1.0, -1.0, -0.5, 0.125, 0.0]):
            V.memset(cst_sb[:, ii:ii + 1], cv)
            nc.const_aps.register(F32, cv, cst_sb[:, ii:ii + 1])
        bnop(s_v, V)
        for eng in (SY, SC, T, G):
            eng.wait_ge(s_v.h, s_v.v)

        slots_mark = sb_scope()
        slotsT_sb = sb("slotsT_sb", [128, 8 * NP], BF16)
        slotsT_v = slotsT_sb[:].rearrange("p (dc np) -> p dc np", dc=8)
        p1_mark_after_slots = sb_scope()
        g1_sb = sb("g1_sb", [128, NMC * NP], BF16)
        el8 = sb("el8", [128, 2 * 8 * 512], BF16)
        rcp1 = sb("rcp1", [128, 512], F32)
        x1t_st = sb("x1t_st", [128, 2 * 512], BF16)
        el8_v = el8[:].rearrange("p (u b c) -> p u b c", u=2, b=8)
        g1_v = g1_sb[:].rearrange("p (mc np) -> p mc np", np=NP)
        x1t_v = x1t_st[:].rearrange("p (u d) -> p u d", u=2)
        mm_done = {0: 0, 1: 0}
        g1_done = {0: 0, 1: 0}
        rcp_done = {0: 0, 1: 0}
        k = 0
        for mc in range(NMC):
            for npc in range(4):
                u = k % 2
                SY.wait_ge(sld.h, sld.v)
                if k >= 2:
                    SY.wait_ge(s_t.h, mm_done[u])
                    SY.wait_ge(s_v.h, g1_done[u])
                for b in range(8):
                    bump(sld, SY.dma_start(
                        out=el8_v[:, u, b, :],
                        in_=el_in[b][128 * mc:128 * mc + 128,
                                     512 * npc:512 * npc + 512]), 16)
                elld_v = sld.v
                pt = pA if u == 0 else pB
                T.wait_ge(sld.h, elld_v)
                if k >= 2:
                    T.wait_ge(s_s.h, rcp_done[u])
                for b in range(8):
                    ins = T.matmul(pt[:], identb_sb[:], el8_v[:, u, b, :],
                                   start=(b == 0), stop=(b == 7))
                ins = bump(s_t, ins)
                mm_done[u] = s_t.v
                SC.wait_ge(s_t.h, s_t.v)
                if k >= 1:
                    SC.wait_ge(s_v.h, s_v.v)
                SC.activation(out=rcp1[:], in_=pt[:], func=AF.Ln)
                ins = SC.activation(out=rcp1[:], in_=rcp1[:], func=AF.Exp, scale=-1.0)
                ins = bnop(s_s, SC)
                rcp_done[u] = s_s.v
                V.wait_ge(s_s.h, s_s.v)
                ins = V.tensor_tensor(out=g1_v[:, mc, 512 * npc:512 * npc + 512],
                                      in0=el8_v[:, u, 0, :], in1=rcp1[:], op=ALU.mult)
                bump(s_v, ins)
                g1_done[u] = s_v.v
                k += 1
        bnop(s_v, V)
        phase_barrier()

        x1_ld = {}
        glc = 0
        for dh2 in range(2):
            for npc in range(4):
                for mc in range(NMC):
                    u = glc % 2
                    SY.wait_ge(sld.h, sld.v)
                    if glc >= 2:
                        SY.wait_ge(s_t.h, x1_ld[glc - 2][1])
                    bump(sld, SY.dma_start(
                        out=x1t_v[:, u, :],
                        in_=x1lntok[128 * mc:128 * mc + 128,
                                    512 * dh2:512 * dh2 + 512]), 16)
                    x1_ld[glc] = (sld.v, None)
                    T.wait_ge(sld.h, x1_ld[glc][0])
                    if mc == 0:
                        T.wait_ge(s_v.h, s_v.v)
                    for dc4 in range(4):
                        pt = (pA, pB, pC, pD)[dc4]
                        ins = T.matmul(pt[:], x1t_v[:, u, 128 * dc4:128 * dc4 + 128],
                                       g1_v[:, mc, 512 * npc:512 * npc + 512],
                                       start=(mc == 0), stop=(mc == NMC - 1))
                    ins = bump(s_t, ins)
                    x1_ld[glc] = (x1_ld[glc][0], s_t.v)
                    glc += 1
                V.wait_ge(s_t.h, s_t.v)
                for dc4 in range(4):
                    pt = (pA, pB, pC, pD)[dc4]
                    ins = V.tensor_copy(
                        slotsT_v[:, dh2 * 4 + dc4, 512 * npc:512 * npc + 512], pt[:])
                bump(s_v, ins)
        bnop(s_v, V)
        SY.wait_ge(s_v.h, s_v.v)
        SY.wait_ge(sst.h, sst.v)
        bump(sst, SY.dma_start(out=g1_sp.rearrange("(mc p) np -> p mc np", p=128),
                               in_=g1_v), 16)
        bump(sst, SY.dma_start(out=slots_sp.rearrange("(dc p) np -> p dc np", p=128),
                               in_=slotsT_v), 16)
        phase_barrier()

        sb_free(p1_mark_after_slots)
        wbuf = sb("wbuf", [128, 2 * 4 * 8 * 512], BF16)
        bstage = sb("bstage", [1, 4 * DM], BF16)
        y1st = sb("y1st", [128, 2 * 512], BF16)
        wbuf_v = wbuf[:].rearrange("p (u kx dc e) -> p u kx dc e", u=2, kx=4, dc=8)
        y1st_v = y1st[:].rearrange("p (u e) -> p u e", u=2)
        w_ld = {}
        ev_done = {}
        y1_dma = {0: 0, 1: 0}
        kk = 0
        for eh in range(2):
            for ng in range(16):
                u = kk % 2
                SY.wait_ge(sld.h, sld.v)
                if kk >= 2:
                    SY.wait_ge(s_t.h, w_ld[kk - 2])
                for kx in range(4):
                    n = ng * 4 + kx
                    bump(sld, SY.dma_start(
                        out=wbuf_v[:, u, kx, :, :],
                        in_=wffn[n, :, 512 * eh:512 * eh + 512]
                            .rearrange("(dc p) e -> p dc e", p=128)), 16)
                bump(sld, SY.dma_start(
                    out=bstage[0:1, :],
                    in_=bffn_row[0:1, ng * 4 * DM:(ng + 1) * 4 * DM]), 16)
                wld_v = sld.v
                pt = pA if u == 0 else pB
                T.wait_ge(sld.h, wld_v)
                if kk >= 2:
                    T.wait_ge(s_s.h, ev_done[kk - 2])
                for kx in range(4):
                    for dc in range(8):
                        T.matmul(pt[32 * kx:32 * kx + 32, :],
                                 slotsT_v[:, dc, (ng * 4 + kx) * 32:(ng * 4 + kx) * 32 + 32],
                                 wbuf_v[:, u, kx, dc, :],
                                 start=(dc == 0), stop=False,
                                 tile_position=(0, 32 * kx))
                    ins = T.matmul(pt[32 * kx:32 * kx + 32, :],
                                   onesr_b[0:1, 0:32],
                                   bstage[0:1, kx * DM + 512 * eh:kx * DM + 512 * eh + 512],
                                   start=False, stop=True, tile_position=(0, 32 * kx))
                ins = bump(s_t, ins)
                w_ld[kk] = s_t.v
                SC.wait_ge(s_t.h, s_t.v)
                if kk >= 2:
                    SC.wait_ge(sst.h, y1_dma[u])
                ins = SC.copy(out=y1st_v[:, u, :], in_=pt[:])
                bump(s_s, ins)
                ev_done[kk] = s_s.v
                SY.wait_ge(s_s.h, s_s.v)
                SY.wait_ge(sst.h, sst.v)
                bump(sst, SY.dma_start(
                    out=y1_sp[:, 512 * eh:512 * eh + 512]
                        .rearrange("(ng p) e -> p ng e", p=128)[:, ng, :],
                    in_=y1st_v[:, u, :]), 16)
                y1_dma[u] = sst.v
                kk += 1
        phase_barrier()

        sb_free(slots_mark)
        p4_mark = sb_scope()
        y1blk = sb("y1blk", [128, 2 * 16 * 128], BF16)
        eT_st = sb("eT_st", [128, 2 * 16 * 512], BF16)
        r1st = sb("r1st", [128, 2 * 512], F32)
        x2keep = sb("x2keep", [128, 8 * 512], F32)
        x2b_st = sb("x2b_st", [128, 512], BF16)
        x2sq_st = sb("x2sq_st", [128, 512], BF16)
        row_mu = sb("row_mu", [1, 512], F32)
        row_var = sb("row_var", [1, 512], F32)
        row_sx = sb("row_sx", [1, 512], F32)
        row_rstd = sb("row_rstd", [1, 512], F32)
        row_mur = sb("row_mur", [1, 512], F32)
        x2ln_asm = sb("x2ln_asm", [128, 8 * 512], BF16)
        y1b_v = y1blk[:].rearrange("p (u nc e) -> p u nc e", u=2, nc=16)
        eT_v = eT_st[:].rearrange("p (u nc m) -> p u nc m", u=2, nc=16)
        r1_vv = r1st[:].rearrange("p (u m) -> p u m", u=2)
        x2k_v = x2keep[:].rearrange("p (ec m) -> p ec m", ec=8)
        x2a_v = x2ln_asm[:].rearrange("p (ec m) -> p ec m", ec=8)
        x2T_v = x2T_sp.rearrange("(ec p) m -> p ec m", p=128)
        x2lnT_v = x2lnT_sp.rearrange("(dc p) m -> p dc m", p=128)
        r1T_v = r1T.rearrange("(ec p) m -> p ec m", p=128)
        eT_ld = {}
        x2dma = 0
        x2_done = {0: 0, 1: 0}
        r1_used = {0: 0, 1: 0}
        y1b_used = {0: 0, 1: 0}
        for mc4 in range(NM4):
            u4 = mc4 % 2
            SY.wait_ge(sld.h, sld.v)
            if mc4 >= 2:
                SY.wait_ge(s_t.h, eT_ld[mc4 - 2])
            bump(sld, SY.dma_start(
                out=eT_v[:, u4, :, :],
                in_=elTsc[:, 512 * mc4:512 * mc4 + 512]
                    .rearrange("(nc p) m -> p nc m", p=128)), 16)
            eTld_v = sld.v
            for ec in range(8):
                ue = ec % 2
                SY.wait_ge(sld.h, sld.v)
                if mc4 > 0 or ec >= 2:
                    SY.wait_ge(s_v.h, r1_used[ue])
                    SY.wait_ge(s_t.h, y1b_used[ue])
                bump(sld, SY.dma_start(out=r1_vv[:, ue, :],
                                       in_=r1T_v[:, ec, 512 * mc4:512 * mc4 + 512]), 16)
                bump(sld, SY.dma_start(
                    out=y1b_v[:, ue, :, :],
                    in_=y1_sp[:, 128 * ec:128 * ec + 128]
                        .rearrange("(nc p) e -> p nc e", p=128)), 16)
                r1ld_v = sld.v
                pt = pA if ue == 0 else pB
                T.wait_ge(sld.h, eTld_v)
                T.wait_ge(sld.h, r1ld_v)
                if mc4 > 0 or ec >= 2:
                    T.wait_ge(s_v.h, x2_done[ue])
                for nco in range(16):
                    ins = T.matmul(pt[:], y1b_v[:, ue, nco, :],
                                   eT_v[:, u4, nco, :],
                                   start=(nco == 0), stop=(nco == 15))
                ins = bump(s_t, ins)
                y1b_used[ue] = s_t.v
                if ec == 7:
                    eT_ld[mc4] = s_t.v
                V.wait_ge(s_t.h, s_t.v)
                V.wait_ge(sld.h, r1ld_v)
                if mc4 > 0:
                    V.wait_ge(sst.h, x2dma)
                ins = V.tensor_tensor(out=x2k_v[:, ec, :], in0=pt[:],
                                      in1=r1_vv[:, ue, :], op=ALU.add)
                bump(s_v, ins)
                x2_done[ue] = s_v.v
                r1_used[ue] = s_v.v
                ins = V.tensor_copy(x2b_st[:], x2k_v[:, ec, :])
                bnop(s_v, V)
                SC.wait_ge(s_v.h, s_v.v)
                ins = SC.activation(out=x2sq_st[:], in_=x2b_st[:], func=AF.Square)
                bnop(s_s, SC)
                T.wait_ge(s_s.h, s_s.v)
                T.matmul(pG[0:1, :], onesc_b[:], x2b_st[:],
                         start=(ec == 0), stop=(ec == 7))
                ins = T.matmul(pH[0:1, :], onesc_b[:], x2sq_st[:],
                               start=(ec == 0), stop=(ec == 7))
                bump(s_t, ins)
            V.wait_ge(s_t.h, s_t.v)
            V.tensor_scalar(out=row_mu[:], in0=pG[0:1, :], scalar1=1.0 / DM,
                            scalar2=None, op0=ALU.mult)
            V.tensor_scalar(out=row_var[:], in0=pH[0:1, :], scalar1=1.0 / DM,
                            scalar2=None, op0=ALU.mult)
            V.tensor_tensor(out=row_sx[:], in0=row_mu[:], in1=row_mu[:], op=ALU.mult)
            V.tensor_tensor(out=row_var[:], in0=row_var[:], in1=row_sx[:],
                            op=ALU.subtract)
            bnop(s_v, V)
            SC.wait_ge(s_v.h, s_v.v)
            SC.activation(out=row_var[:], in_=row_var[:], func=AF.Relu)
            SC.activation(out=row_var[:], in_=row_var[:], func=AF.Ln, bias=LN_EPS)
            SC.activation(out=row_rstd[:], in_=row_var[:], func=AF.Exp, scale=-0.5)
            bnop(s_s, SC)
            V.wait_ge(s_s.h, s_s.v)
            V.tensor_tensor(out=row_mur[:], in0=row_mu[:], in1=row_rstd[:], op=ALU.mult)
            bnop(s_v, V)
            T.wait_ge(s_v.h, s_v.v)
            T.matmul(pE[:], onesr_f[:], row_rstd[:], start=True, stop=True)
            ins = T.matmul(pF[:], onesr_f[:], row_mur[:], start=True, stop=True)
            bump(s_t, ins)
            V.wait_ge(s_t.h, s_t.v)
            if mc4 > 0:
                V.wait_ge(sst.h, x2dma)
            for ec in range(8):
                V.tensor_tensor(out=x2a_v[:, ec, :], in0=x2k_v[:, ec, :], in1=pE[:],
                                op=ALU.mult)
                V.tensor_tensor(out=x2a_v[:, ec, :], in0=x2a_v[:, ec, :], in1=pF[:],
                                op=ALU.subtract)
            bnop(s_v, V)
            SY.wait_ge(s_v.h, s_v.v)
            SY.wait_ge(sst.h, sst.v)
            bump(sst, SY.dma_start(out=x2T_v[:, :, 512 * mc4:512 * mc4 + 512],
                                   in_=x2k_v), 16)
            bump(sst, SY.dma_start(out=x2lnT_v[:, :, 512 * mc4:512 * mc4 + 512],
                                   in_=x2a_v), 16)
            x2dma = sst.v
        phase_barrier()

        sb_free(p4_mark)
        p5_mark = sb_scope()
        k_sb = sb("k_sb", [128, 8 * M], BF16)
        p5a_mark = sb_scope()
        wqk_sb = sb("wqk_sb", [128, 8 * 2048], BF16)
        bq_sb = sb("bq_sb", [128, 16], F32)
        x2st = sb("x2st", [128, 2 * 8 * 512], BF16)
        qst = sb("qst", [128, 2 * 512], BF16)
        k_v = k_sb[:].rearrange("p (eb m) -> p eb m", eb=8)
        wqk_v = wqk_sb[:].rearrange("p (dc e) -> p dc e", dc=8)
        x2s_v = x2st[:].rearrange("p (u dc m) -> p u dc m", u=2, dc=8)
        qst_v = qst[:].rearrange("p (u m) -> p u m", u=2)
        bump(sld, SY.dma_start(out=wqk_v[:, :, :],
                               in_=wqkv[:, 0:2048].rearrange("(dc p) e -> p dc e", p=128)), 16)
        bump(sld, SY.dma_start(out=bq_sb[:],
                               in_=bqkv_col[0:2048].rearrange("(eb p) o -> p (eb o)", p=128),
                               allow_slow_non_contiguous=True), 16)
        wq_ld = sld.v
        T.wait_ge(sld.h, wq_ld)
        SC.wait_ge(sld.h, wq_ld)
        x2_ld = {}
        q_dma = {0: 0, 1: 0}
        qk_ev = {0: 0, 1: 0}
        for tc in range(NTC):
            ut = tc % 2
            SY.wait_ge(sld.h, sld.v)
            if tc >= 2:
                SY.wait_ge(s_t.h, x2_ld[tc - 2])
            bump(sld, SY.dma_start(
                out=x2s_v[:, ut, :, :],
                in_=x2lnT_v[:, :, TCH * tc:TCH * tc + TCH]), 16)
            x2ld_v = sld.v
            T.wait_ge(sld.h, x2ld_v)
            for eb in range(16):
                ue = eb % 2
                pt = pA if ue == 0 else pB
                if tc > 0 or eb >= 2:
                    T.wait_ge(s_v.h, qk_ev[ue])
                for dc in range(8):
                    ins = T.matmul(pt[:], wqk_v[:, dc, 128 * eb:128 * eb + 128],
                                   x2s_v[:, ut, dc, :],
                                   start=(dc == 0), stop=(dc == 7))
                ins = bump(s_t, ins)
                if eb == 15:
                    x2_ld[tc] = s_t.v
                V.wait_ge(s_t.h, s_t.v)
                if eb < 8:
                    if tc > 0 or eb >= 2:
                        V.wait_ge(sst.h, q_dma[ue])
                    ins = V.tensor_scalar(out=qst_v[:, ue, :], in0=pt[:],
                                          scalar1=bq_sb[:, eb:eb + 1], scalar2=None,
                                          op0=ALU.add)
                    ins = bnop(s_v, V)
                    SY.wait_ge(s_v.h, s_v.v)
                    SY.wait_ge(sst.h, sst.v)
                    bump(sst, SY.dma_start(
                        out=qT_sp[128 * eb:128 * eb + 128, TCH * tc:TCH * tc + TCH],
                        in_=qst_v[:, ue, :]), 16)
                    q_dma[ue] = sst.v
                else:
                    ins = V.tensor_scalar(out=k_v[:, eb - 8, TCH * tc:TCH * tc + TCH],
                                          in0=pt[:],
                                          scalar1=bq_sb[:, eb:eb + 1], scalar2=None,
                                          op0=ALU.add)
                    bump(s_v, ins)
                qk_ev[ue] = s_v.v
        bnop(s_v, V)
        phase_barrier()

        sb_free(p5a_mark)
        vext_sb = sb("vext_sb", [128, (M // 128) * 1040], BF16)
        p5b_mark = sb_scope()
        wv_sb = sb("wv_sb", [128, 8 * DM], BF16)
        bv_st = sb("bv_st", [1, DM], BF16)
        x2s2 = sb("x2s2", [128, 2 * 8 * 128], BF16)
        vext_v = vext_sb[:].rearrange("p (tc e) -> p tc e", tc=M // 128)
        wv_v = wv_sb[:].rearrange("p (dc e) -> p dc e", dc=8)
        x22_v = x2s2[:].rearrange("p (u dc m) -> p u dc m", u=2, dc=8)
        bump(sld, SY.dma_start(out=wv_v[:, :, :],
                               in_=wqkv[:, 2048:3072].rearrange("(dc p) e -> p dc e", p=128)), 16)
        bump(sld, SY.dma_start(out=bv_st[:], in_=bv_row), 16)
        wv_ld = sld.v
        T.wait_ge(sld.h, wv_ld)
        V.wait_ge(sld.h, wv_ld)
        NT32 = M // 128
        x22_ld = {}
        vev_done = {0: 0, 1: 0}
        for t32 in range(NT32):
            u = t32 % 2
            SY.wait_ge(sld.h, sld.v)
            if t32 >= 2:
                SY.wait_ge(s_t.h, x22_ld[t32 - 2])
            bump(sld, SY.dma_start(
                out=x22_v[:, u, :, :],
                in_=x2lnT_v[:, :, 128 * t32:128 * t32 + 128]), 16)
            xld_v = sld.v
            V.memset(vext_v[:, t32, :], 1.0)
            bnop(s_v, V)
            T.wait_ge(sld.h, xld_v)
            T.wait_ge(s_v.h, s_v.v)
            for ehalf in range(2):
                pt = pA if ehalf == 0 else pB
                if t32 > 0:
                    T.wait_ge(s_v.h, vev_done[ehalf])
                for dc in range(8):
                    T.matmul(pt[:], x22_v[:, u, dc, :],
                             wv_v[:, dc, 512 * ehalf:512 * ehalf + 512],
                             start=(dc == 0), stop=False)
                ins = T.matmul(pt[:], onesr_b[:],
                               bv_st[0:1, 512 * ehalf:512 * ehalf + 512],
                               start=False, stop=True)
                ins = bump(s_t, ins)
                if ehalf == 1:
                    x22_ld[t32] = s_t.v
                V.wait_ge(s_t.h, s_t.v)
                for i in range(8):
                    h = ehalf * 8 + i
                    ins = V.tensor_copy(vext_v[:, t32, 65 * h:65 * h + 64],
                                        pt[:, 64 * i:64 * i + 64])
                ins = bump(s_v, ins)
                vev_done[ehalf] = s_v.v
        bnop(s_v, V)
        SY.wait_ge(s_v.h, s_v.v)
        SY.wait_ge(sst.h, sst.v)
        bump(sst, SY.dma_start(out=k_sp.rearrange("(eb p) m -> p eb m", p=128),
                               in_=k_v), 16)
        bump(sst, SY.dma_start(out=vext_sp.rearrange("(tc p) e -> p tc e", p=128),
                               in_=vext_v), 16)
        phase_barrier()

        sb_free(p5b_mark)
        q_sb = sb("q_sb", [128, 8 * TCH], BF16)
        pb_sb = sb("pb_sb", [128, 2 * TCH], BF16)
        diagm_sb = sb("diagm_sb", [128, 4 * 512], BF16)
        ctx_asm = sb("ctx_asm", [128, 8 * TCH], BF16)
        rsrow = sb("rsrow", [1, 4 * TCH], F32)
        q_v = q_sb[:].rearrange("p (eb m) -> p eb m", eb=8)
        pb_v = pb_sb[:].rearrange("p (u m) -> p u m", u=2)
        ctx_v = ctx_asm[:].rearrange("p (eb m) -> p eb m", eb=8)
        bump(sld, SY.dma_start(out=diagm_sb[:], in_=diagm_in), 16)
        dm_ld = sld.v
        V.wait_ge(sld.h, dm_ld)
        ctx_dma = 0
        exp_done = {0: 0, 1: 0}
        pb_used = {0: 0, 1: 0}
        for tc in range(NTC):
            SY.wait_ge(sld.h, sld.v)
            SY.wait_ge(sst.h, sst.v)
            SY.wait_ge(s_t.h, s_t.v)
            bump(sld, SY.dma_start(
                out=q_v[:, :, :],
                in_=qT_sp.rearrange("(eb p) m -> p eb m", p=128)[:, :, TCH * tc:TCH * tc + TCH]), 16)
            q_ld = sld.v
            T.wait_ge(sld.h, q_ld)
            if tc > 0:
                SC.wait_ge(sst.h, ctx_dma)
            nsc = 4 * (tc + 1)
            pb_cnt = {0: 0, 1: 0}
            for hg in range(4):
                pctx = (pA, pB, pC, pD)
                for sc_ in range(nsc):
                    diag_off = sc_ - 4 * tc
                    for hh in range(4):
                        h = hg * 4 + hh
                        us = (sc_ * 4 + hh) % 2
                        pse = pE if us == 0 else pF
                        rb = 64 * (h % 2)
                        if exp_done[us] > 0:
                            T.wait_ge(s_s.h, exp_done[us])
                        if sc_ == 0:
                            T.wait_ge(s_v.h, s_v.v)
                        ins = T.matmul(pse[:], k_v[rb:rb + 64, h // 2,
                                                   128 * sc_:128 * sc_ + 128],
                                       q_v[rb:rb + 64, h // 2, :],
                                       start=True, stop=True)
                        bump(s_t, ins)
                        SC.wait_ge(s_t.h, s_t.v)
                        if pb_used[us] > 0:
                            SC.wait_ge(s_t.h, pb_used[us])
                        ins = SC.activation(out=pb_v[:, us, :], in_=pse[:],
                                            func=AF.Exp, scale=0.125)
                        ins = bump(s_s, ins)
                        exp_done[us] = s_s.v
                        if diag_off >= 0:
                            V.wait_ge(s_s.h, s_s.v)
                            ins = V.tensor_tensor(
                                out=pb_v[:, us, :], in0=pb_v[:, us, :],
                                in1=diagm_sb[:, 512 * diag_off:512 * diag_off + 512],
                                op=ALU.mult)
                            ins = bnop(s_v, V)
                            T.wait_ge(s_v.h, s_v.v)
                        else:
                            T.wait_ge(s_s.h, s_s.v)
                        ins = T.matmul(pctx[hh][0:65, :],
                                       vext_v[:, sc_, 65 * h:65 * h + 65],
                                       pb_v[:, us, :],
                                       start=(sc_ == 0), stop=(sc_ == nsc - 1))
                        ins = bump(s_t, ins)
                        pb_used[us] = s_t.v
                        pb_cnt[us] += 1
                SC.wait_ge(s_t.h, s_t.v)
                for hh in range(4):
                    h = hg * 4 + hh
                    rb = 64 * (h % 2)
                    SC.copy(out=ctx_v[rb:rb + 64, h // 2, :], in_=pctx[hh][0:64, :])
                    ins = SC.copy(out=rsrow[0:1, TCH * hh:TCH * hh + TCH],
                                  in_=pctx[hh][64:65, :])
                SC.activation(out=rsrow[:], in_=rsrow[:], func=AF.Ln)
                ins = SC.activation(out=rsrow[:], in_=rsrow[:], func=AF.Exp, scale=-1.0)
                bnop(s_s, SC)
                for hh in range(4):
                    h = hg * 4 + hh
                    rb = 64 * (h % 2)
                    T.wait_ge(s_s.h, s_s.v)
                    if hh > 0:
                        T.wait_ge(s_v.h, s_v.v)
                    ins = T.matmul(pG[0:64, :], onesr_f[0:1, 0:64],
                                   rsrow[0:1, TCH * hh:TCH * hh + TCH],
                                   start=True, stop=True)
                    bump(s_t, ins)
                    V.wait_ge(s_t.h, s_t.v)
                    ins = V.tensor_tensor(out=ctx_v[rb:rb + 64, h // 2, :],
                                          in0=ctx_v[rb:rb + 64, h // 2, :],
                                          in1=pG[0:64, :], op=ALU.mult)
                    bump(s_v, ins)
                bnop(s_v, V)
            SY.wait_ge(s_v.h, s_v.v)
            SY.wait_ge(sst.h, sst.v)
            bump(sst, SY.dma_start(
                out=ctxT_sp.rearrange("(eb p) m -> p eb m", p=128)[:, :, TCH * tc:TCH * tc + TCH],
                in_=ctx_v), 16)
            ctx_dma = sst.v
        phase_barrier()

        sb_free(p5_mark)
        p7b_mark = sb_scope()
        wout_sb = sb("wout_sb", [128, 8 * DM], BF16)
        bout_st = sb("bout_st", [1, DM], BF16)
        cst = sb("cst7", [128, 2 * 8 * 512], BF16)
        x2st7 = sb("x2st7", [128, 2 * 512], F32)
        x3keep = sb("x3keep", [128, 8 * 512], F32)
        x3b_st = sb("x3b_st", [128, 512], BF16)
        x3sq_st = sb("x3sq_st", [128, 512], BF16)
        row_mu7 = sb("row_mu7", [1, 512], F32)
        row_var7 = sb("row_var7", [1, 512], F32)
        row_sx7 = sb("row_sx7", [1, 512], F32)
        row_rstd7 = sb("row_rstd7", [1, 512], F32)
        row_mur7 = sb("row_mur7", [1, 512], F32)
        x3ln_asm = sb("x3ln_asm", [128, 8 * 512], BF16)
        wout_v = wout_sb[:].rearrange("p (dc e) -> p dc e", dc=8)
        cst_v = cst[:].rearrange("p (u dc m) -> p u dc m", u=2, dc=8)
        x2s7_v = x2st7[:].rearrange("p (u m) -> p u m", u=2)
        x3k_v = x3keep[:].rearrange("p (ec m) -> p ec m", ec=8)
        x3a_v = x3ln_asm[:].rearrange("p (ec m) -> p ec m", ec=8)
        x3T_v = x3T_out.rearrange("(ec p) m -> p ec m", p=128)
        x3lnT_v = x3lnT_out.rearrange("(dc p) m -> p dc m", p=128)
        bump(sld, SY.dma_start(out=wout_v[:, :, :],
                               in_=wout_in.rearrange("(dc p) e -> p dc e", p=128)), 16)
        bump(sld, SY.dma_start(out=bout_st[:], in_=bout_row), 16)
        wo_ld = sld.v
        T.wait_ge(sld.h, wo_ld)
        ctx_ld = {}
        x3dma = 0
        x3_done = {0: 0, 1: 0}
        x2u7 = {0: 0, 1: 0}
        for tc in range(NTC):
            ut = tc % 2
            SY.wait_ge(sld.h, sld.v)
            if tc >= 2:
                SY.wait_ge(s_t.h, ctx_ld[tc - 2])
            bump(sld, SY.dma_start(
                out=cst_v[:, ut, :, :],
                in_=ctxT_sp.rearrange("(dc p) m -> p dc m", p=128)[:, :, TCH * tc:TCH * tc + TCH]), 16)
            ctxld_v = sld.v
            T.wait_ge(sld.h, ctxld_v)
            for ec in range(8):
                ue = ec % 2
                SY.wait_ge(sld.h, sld.v)
                if tc > 0 or ec >= 2:
                    SY.wait_ge(s_v.h, x2u7[ue])
                bump(sld, SY.dma_start(out=x2s7_v[:, ue, :],
                                       in_=x2T_v[:, ec, TCH * tc:TCH * tc + TCH]), 16)
                x2ld7 = sld.v
                pt = pA if ue == 0 else pB
                if tc > 0 or ec >= 2:
                    T.wait_ge(s_v.h, x3_done[ue])
                for dc in range(8):
                    T.matmul(pt[:], wout_v[:, dc, 128 * ec:128 * ec + 128],
                             cst_v[:, ut, dc, :], start=(dc == 0), stop=False)
                ins = T.matmul(pt[:], bout_st[0:1, 128 * ec:128 * ec + 128],
                               ones512_b[0:1, :], start=False, stop=True)
                ins = bump(s_t, ins)
                if ec == 7:
                    ctx_ld[tc] = s_t.v
                V.wait_ge(s_t.h, s_t.v)
                V.wait_ge(sld.h, x2ld7)
                if tc > 0:
                    V.wait_ge(sst.h, x3dma)
                ins = V.tensor_tensor(out=x3k_v[:, ec, :], in0=pt[:],
                                      in1=x2s7_v[:, ue, :], op=ALU.add)
                bump(s_v, ins)
                x3_done[ue] = s_v.v
                x2u7[ue] = s_v.v
                ins = V.tensor_copy(x3b_st[:], x3k_v[:, ec, :])
                bnop(s_v, V)
                SC.wait_ge(s_v.h, s_v.v)
                ins = SC.activation(out=x3sq_st[:], in_=x3b_st[:], func=AF.Square)
                bnop(s_s, SC)
                T.wait_ge(s_s.h, s_s.v)
                T.matmul(pG[0:1, :], onesc_b[:], x3b_st[:],
                         start=(ec == 0), stop=(ec == 7))
                ins = T.matmul(pH[0:1, :], onesc_b[:], x3sq_st[:],
                               start=(ec == 0), stop=(ec == 7))
                bump(s_t, ins)
            V.wait_ge(s_t.h, s_t.v)
            V.tensor_scalar(out=row_mu7[:], in0=pG[0:1, :], scalar1=1.0 / DM,
                            scalar2=None, op0=ALU.mult)
            V.tensor_scalar(out=row_var7[:], in0=pH[0:1, :], scalar1=1.0 / DM,
                            scalar2=None, op0=ALU.mult)
            V.tensor_tensor(out=row_sx7[:], in0=row_mu7[:], in1=row_mu7[:], op=ALU.mult)
            V.tensor_tensor(out=row_var7[:], in0=row_var7[:], in1=row_sx7[:],
                            op=ALU.subtract)
            bnop(s_v, V)
            SC.wait_ge(s_v.h, s_v.v)
            SC.activation(out=row_var7[:], in_=row_var7[:], func=AF.Relu)
            SC.activation(out=row_var7[:], in_=row_var7[:], func=AF.Ln, bias=LN_EPS)
            SC.activation(out=row_rstd7[:], in_=row_var7[:], func=AF.Exp, scale=-0.5)
            bnop(s_s, SC)
            V.wait_ge(s_s.h, s_s.v)
            V.tensor_tensor(out=row_mur7[:], in0=row_mu7[:], in1=row_rstd7[:],
                            op=ALU.mult)
            bnop(s_v, V)
            T.wait_ge(s_v.h, s_v.v)
            T.matmul(pE[:], onesr_f[:], row_rstd7[:], start=True, stop=True)
            ins = T.matmul(pF[:], onesr_f[:], row_mur7[:], start=True, stop=True)
            bump(s_t, ins)
            V.wait_ge(s_t.h, s_t.v)
            if tc > 0:
                V.wait_ge(sst.h, x3dma)
            for ec in range(8):
                V.tensor_tensor(out=x3a_v[:, ec, :], in0=x3k_v[:, ec, :], in1=pE[:],
                                op=ALU.mult)
                V.tensor_tensor(out=x3a_v[:, ec, :], in0=x3a_v[:, ec, :], in1=pF[:],
                                op=ALU.subtract)
            bnop(s_v, V)
            SY.wait_ge(s_v.h, s_v.v)
            SY.wait_ge(sst.h, sst.v)
            bump(sst, SY.dma_start(out=x3T_v[:, :, TCH * tc:TCH * tc + TCH],
                                   in_=x3k_v), 16)
            bump(sst, SY.dma_start(out=x3lnT_v[:, :, TCH * tc:TCH * tc + TCH],
                                   in_=x3a_v), 16)
            x3dma = sst.v
        phase_barrier()

        sb_free(p7b_mark)
        phi_sb = sb("phi_sb", [128, 8 * NP], BF16)
        l8 = sb("l8", [128, 2 * 8 * 128], BF16)
        r8 = sb("r8", [128, 2 * 8 * 512], BF16)
        e2st = sb("e2st", [128, 2 * 512], BF16)
        rs4 = sb("rs4", [128, 4], F32)
        rsc = sb("rsc", [128, 1], F32)
        lnrs_all = sb("lnrs_all", [128, NMC], F32)
        lnhi_b = sb("lnhi_b", [128, NMC], BF16)
        lnhi32 = sb("lnhi32", [128, NMC], F32)
        lnlo_b = sb("lnlo_b", [128, NMC], BF16)
        negrow = sb("negrow", [1, 2 * 512], BF16)
        phi_v = phi_sb[:].rearrange("p (dc np) -> p dc np", dc=8)
        l8_v = l8[:].rearrange("p (u dc m) -> p u dc m", u=2, dc=8)
        r8_v = r8[:].rearrange("p (u dc m) -> p u dc m", u=2, dc=8)
        e2_v = e2st[:].rearrange("p (u np) -> p u np", u=2)
        bump(sld, SY.dma_start(out=phi_v[:, :, :],
                               in_=phi.rearrange("(dc p) np -> p dc np", p=128)), 16)
        phi_ld = sld.v
        T.wait_ge(sld.h, phi_ld)
        l_ld = {}
        e2dma = {0: 0, 1: 0}
        e2_used = {0: 0, 1: 0}
        for mc in range(NMC):
            u = mc % 2
            SY.wait_ge(sld.h, sld.v)
            if mc >= 2:
                SY.wait_ge(s_t.h, l_ld[mc - 2])
            bump(sld, SY.dma_start(
                out=l8_v[:, u, :, :],
                in_=x3lnT_v[:, :, 128 * mc:128 * mc + 128]), 16)
            lld_v = sld.v
            T.wait_ge(sld.h, lld_v)
            for npc in range(4):
                un = npc % 2
                pt = pA if un == 0 else pB
                if e2_used[un] > 0:
                    T.wait_ge(s_v.h, e2_used[un])
                for dc in range(8):
                    ins = T.matmul(pt[:], l8_v[:, u, dc, :],
                                   phi_v[:, dc, 512 * npc:512 * npc + 512],
                                   start=(dc == 0), stop=(dc == 7))
                ins = bump(s_t, ins)
                if npc == 3:
                    l_ld[mc] = s_t.v
                SC.wait_ge(s_t.h, s_t.v)
                if e2dma[un] > 0:
                    SC.wait_ge(sst.h, e2dma[un])
                ins = SC.activation(out=e2_v[:, un, :], in_=pt[:], func=AF.Exp)
                bnop(s_s, SC)
                V.wait_ge(s_s.h, s_s.v)
                ins = V.tensor_reduce(out=rs4[:, npc:npc + 1], in_=e2_v[:, un, :],
                                      axis=AX.X, op=ALU.add)
                bump(s_v, ins)
                e2_used[un] = s_v.v
                SY.wait_ge(s_v.h, s_v.v)
                SY.wait_ge(sst.h, sst.v)
                bump(sst, SY.dma_start(
                    out=el2_out[128 * mc:128 * mc + 128, 512 * npc:512 * npc + 512],
                    in_=e2_v[:, un, :]), 16)
                e2dma[un] = sst.v
            V.drain()
            V.tensor_reduce(out=rsc[:], in_=rs4[:], axis=AX.X, op=ALU.add)
            bnop(s_v, V)
            SC.wait_ge(s_v.h, s_v.v)
            ins = SC.activation(out=lnrs_all[:, mc:mc + 1], in_=rsc[:], func=AF.Ln)
            bump(s_s, ins)
        V.wait_ge(s_s.h, s_s.v)
        V.tensor_copy(lnhi_b[:], lnrs_all[:])
        V.drain()
        V.tensor_copy(lnhi32[:], lnhi_b[:])
        V.drain()
        V.tensor_tensor(out=lnlo_b[:], in0=lnrs_all[:], in1=lnhi32[:], op=ALU.subtract)
        bnop(s_v, V)
        T.wait_ge(s_v.h, s_v.v)
        r_ld = {}
        eT2dma = {0: 0, 1: 0}
        eT2_ev = {0: 0, 1: 0}
        for mc4 in range(NM4):
            u = mc4 % 2
            for sub in range(4):
                mc = mc4 * 4 + sub
                for hl, src in ((0, lnhi_b), (1, lnlo_b)):
                    T.wait_ge(s_s.h, s_s.v)
                    ins = T.matmul(pG[0:1, 0:128], src[:, mc:mc + 1], identb_sb[:],
                                   start=True, stop=True)
                    bump(s_t, ins)
                    SC.wait_ge(s_t.h, s_t.v)
                    ins = SC.activation(
                        out=negrow[0:1, 512 * hl + 128 * sub:512 * hl + 128 * sub + 128],
                        in_=pG[0:1, 0:128], func=AF.Copy, scale=-1.0)
                    bump(s_s, ins)
            SY.wait_ge(sld.h, sld.v)
            if mc4 >= 2:
                SY.wait_ge(s_t.h, r_ld[mc4 - 2])
            bump(sld, SY.dma_start(
                out=r8_v[:, u, :, :],
                in_=x3lnT_v[:, :, 512 * mc4:512 * mc4 + 512]), 16)
            rld_v = sld.v
            T.wait_ge(sld.h, rld_v)
            T.wait_ge(s_s.h, s_s.v)
            for npb in range(16):
                un = npb % 2
                pt = pA if un == 0 else pB
                if eT2_ev[un] > 0:
                    T.wait_ge(s_s.h, eT2_ev[un])
                for dc in range(8):
                    T.matmul(pt[:], phi_v[:, dc, 128 * npb:128 * npb + 128],
                             r8_v[:, u, dc, :], start=(dc == 0), stop=False)
                T.matmul(pt[:], onesr_b[:], negrow[0:1, 0:512],
                         start=False, stop=False)
                ins = T.matmul(pt[:], onesr_b[:], negrow[0:1, 512:1024],
                               start=False, stop=True)
                ins = bump(s_t, ins)
                if npb == 15:
                    r_ld[mc4] = s_t.v
                SC.wait_ge(s_t.h, s_t.v)
                if eT2dma[un] > 0:
                    SC.wait_ge(sst.h, eT2dma[un])
                ins = SC.activation(out=e2_v[:, un, :], in_=pt[:], func=AF.Exp)
                bump(s_s, ins)
                eT2_ev[un] = s_s.v
                SY.wait_ge(s_s.h, s_s.v)
                SY.wait_ge(sst.h, sst.v)
                bump(sst, SY.dma_start(
                    out=el2Tsc_out[128 * npb:128 * npb + 128, 512 * mc4:512 * mc4 + 512],
                    in_=e2_v[:, un, :]), 16)
                eT2dma[un] = sst.v
        SY.wait_ge(sst.h, sst.v)

    for cm in reversed(ctx):
        cm.__exit__(None, None, None)
    return nc



def build_L3():
    nc = bass.Bass("TRN2", target_bir_lowering=False, detect_race_conditions=False)

    x3T = nc.dram_tensor("x3T", [DM, M], F32, kind="ExternalInput").ap()
    x3lnT = nc.dram_tensor("x3lnT", [DM, M], BF16, kind="ExternalInput").ap()
    el_in = [nc.dram_tensor(f"el_in{b}", [M, NP], BF16, kind="ExternalInput").ap()
             for b in range(8)]
    elTsc = nc.dram_tensor("elTsc", [NP, M], BF16, kind="ExternalInput").ap()
    wffn = nc.dram_tensor("wffn", [NEXP, DM, DM], BF16, kind="ExternalInput").ap()
    bffn_row = nc.dram_tensor("bffn_row", [1, NEXP * DM], BF16, kind="ExternalInput").ap()
    identb = nc.dram_tensor("identb", [128, 128], BF16, kind="ExternalInput").ap()
    onesrb_in = nc.dram_tensor("onesrb", [1, 128], BF16, kind="ExternalInput").ap()

    out_T = nc.dram_tensor("outT", [DM, M], F32, kind="ExternalOutput").ap()

    x3tok_sp = nc.dram_tensor("x3tok_sp", [M, DM], BF16, kind="Internal").ap()
    y2_sp = nc.dram_tensor("y2_sp", [NP, DM], BF16, kind="Internal").ap()

    NMC = M // 128
    NM4 = M // 512

    ctx = []

    def sb(name, shape, dt):
        cm = nc.sbuf_tensor(name, shape, dt)
        t = cm.__enter__()
        ctx.append(cm)
        return t

    def psum(name, shape=(128, 512), dt=F32):
        cm = nc.psum_tensor(name, list(shape), dt)
        t = cm.__enter__()
        ctx.append(cm)
        return t

    def sb_scope():
        return len(ctx)

    def sb_free(mark):
        while len(ctx) > mark:
            ctx.pop().__exit__(None, None, None)

    pA = psum("pA")
    pB = psum("pB")
    pC = psum("pC")
    pD = psum("pD")

    sld = S(nc, "sld")
    sst = S(nc, "sst")
    s_t = S(nc, "s_t")
    s_v = S(nc, "s_v")
    s_s = S(nc, "s_s")
    s_ph = S(nc, "s_ph")

    SY = nc.sync
    V = nc.vector
    SC = nc.scalar
    T = nc.tensor
    G = nc.gpsimd

    def phase_barrier():
        for eng in (SY, V, SC, T, G):
            bnop(s_ph, eng)
        for eng in (SY, V, SC, T, G):
            eng.wait_ge(s_ph.h, s_ph.v)

    onesr_b = sb("onesr_b", [1, 128], BF16)
    identb_sb = sb("identb_sb", [128, 128], BF16)
    cst_sb = sb("cst_sb", [128, 8], F32)

    with nc.Block() as _block:
        for dst, src in [(onesr_b[:], onesrb_in), (identb_sb[:], identb)]:
            bump(sld, SY.dma_start(out=dst, in_=src), 16)
        for ii, cv in enumerate([1.0, -1.0, 0.0]):
            V.memset(cst_sb[:, ii:ii + 1], cv)
            nc.const_aps.register(F32, cv, cst_sb[:, ii:ii + 1])
        bnop(s_v, V)
        for eng in (SY, SC, T, G):
            eng.wait_ge(s_v.h, s_v.v)
        x3lnT_v = x3lnT.rearrange("(dc p) m -> p dc m", p=128)

        q0_mark = sb_scope()
        l8 = sb("l8", [128, 2 * 8 * 128], BF16)
        ttok = sb("ttok", [128, 2 * DM], BF16)
        l8_v = l8[:].rearrange("p (u dc m) -> p u dc m", u=2, dc=8)
        ttok_v = ttok[:].rearrange("p (u d) -> p u d", u=2)
        t_ld = {}
        ttok_dma = {0: 0, 1: 0}
        for mc in range(NMC):
            u = mc % 2
            SY.wait_ge(sld.h, sld.v)
            if mc >= 2:
                SY.wait_ge(s_t.h, t_ld[mc - 2])
            bump(sld, SY.dma_start(
                out=l8_v[:, u, :, :],
                in_=x3lnT_v[:, :, 128 * mc:128 * mc + 128]), 16)
            lld_v = sld.v
            T.wait_ge(sld.h, lld_v)
            for dc in range(8):
                pt = pA if dc % 2 == 0 else pB
                if mc > 0 or dc >= 2:
                    T.wait_ge(s_v.h, tev_done[dc % 2])
                ins = T.matmul(pt[0:128, 0:128], l8_v[:, u, dc, :], identb_sb[:],
                               start=True, stop=True)
                ins = bump(s_t, ins)
                if dc == 7:
                    t_ld[mc] = s_t.v
                V.wait_ge(s_t.h, s_t.v)
                if mc >= 2 and dc < 2:
                    V.wait_ge(sst.h, ttok_dma[u])
                ins = V.tensor_copy(ttok_v[:, u, 128 * dc:128 * dc + 128],
                                    pt[0:128, 0:128])
                ins = bump(s_v, ins)
                tev_done = tev_done if (mc or dc) else {0: 0, 1: 0}
                tev_done[dc % 2] = s_v.v
            bnop(s_v, V)
            SY.wait_ge(s_v.h, s_v.v)
            SY.wait_ge(sst.h, sst.v)
            bump(sst, SY.dma_start(out=x3tok_sp[128 * mc:128 * mc + 128, :],
                                   in_=ttok_v[:, u, :]), 16)
            ttok_dma[u] = sst.v
        phase_barrier()

        sb_free(q0_mark)
        slots_mark = sb_scope()
        slotsT_sb = sb("slotsT_sb", [128, 8 * NP], BF16)
        slotsT_v = slotsT_sb[:].rearrange("p (dc np) -> p dc np", dc=8)
        q1_mark = sb_scope()
        g1_sb = sb("g1_sb", [128, NMC * NP], BF16)
        el8 = sb("el8", [128, 2 * 8 * 512], BF16)
        rcp1 = sb("rcp1", [128, 512], F32)
        x1t_st = sb("x1t_st", [128, 2 * 512], BF16)
        el8_v = el8[:].rearrange("p (u b c) -> p u b c", u=2, b=8)
        g1_v = g1_sb[:].rearrange("p (mc np) -> p mc np", np=NP)
        x1t_v = x1t_st[:].rearrange("p (u d) -> p u d", u=2)
        mm_done = {0: 0, 1: 0}
        g1_done = {0: 0, 1: 0}
        rcp_done = {0: 0, 1: 0}
        k = 0
        for mc in range(NMC):
            for npc in range(4):
                u = k % 2
                SY.wait_ge(sld.h, sld.v)
                if k >= 2:
                    SY.wait_ge(s_t.h, mm_done[u])
                    SY.wait_ge(s_v.h, g1_done[u])
                for b in range(8):
                    bump(sld, SY.dma_start(
                        out=el8_v[:, u, b, :],
                        in_=el_in[b][128 * mc:128 * mc + 128,
                                     512 * npc:512 * npc + 512]), 16)
                elld_v = sld.v
                pt = pA if u == 0 else pB
                T.wait_ge(sld.h, elld_v)
                if k >= 2:
                    T.wait_ge(s_s.h, rcp_done[u])
                for b in range(8):
                    ins = T.matmul(pt[:], identb_sb[:], el8_v[:, u, b, :],
                                   start=(b == 0), stop=(b == 7))
                ins = bump(s_t, ins)
                mm_done[u] = s_t.v
                SC.wait_ge(s_t.h, s_t.v)
                if k >= 1:
                    SC.wait_ge(s_v.h, s_v.v)
                SC.activation(out=rcp1[:], in_=pt[:], func=AF.Ln)
                ins = SC.activation(out=rcp1[:], in_=rcp1[:], func=AF.Exp, scale=-1.0)
                ins = bnop(s_s, SC)
                rcp_done[u] = s_s.v
                V.wait_ge(s_s.h, s_s.v)
                ins = V.tensor_tensor(out=g1_v[:, mc, 512 * npc:512 * npc + 512],
                                      in0=el8_v[:, u, 0, :], in1=rcp1[:], op=ALU.mult)
                bump(s_v, ins)
                g1_done[u] = s_v.v
                k += 1
        bnop(s_v, V)
        phase_barrier()

        x1_ld = {}
        glc = 0
        for dh2 in range(2):
            for npc in range(4):
                for mc in range(NMC):
                    u = glc % 2
                    SY.wait_ge(sld.h, sld.v)
                    if glc >= 2:
                        SY.wait_ge(s_t.h, x1_ld[glc - 2][1])
                    bump(sld, SY.dma_start(
                        out=x1t_v[:, u, :],
                        in_=x3tok_sp[128 * mc:128 * mc + 128,
                                     512 * dh2:512 * dh2 + 512]), 16)
                    x1_ld[glc] = (sld.v, None)
                    T.wait_ge(sld.h, x1_ld[glc][0])
                    if mc == 0:
                        T.wait_ge(s_v.h, s_v.v)
                    for dc4 in range(4):
                        pt = (pA, pB, pC, pD)[dc4]
                        ins = T.matmul(pt[:], x1t_v[:, u, 128 * dc4:128 * dc4 + 128],
                                       g1_v[:, mc, 512 * npc:512 * npc + 512],
                                       start=(mc == 0), stop=(mc == NMC - 1))
                    ins = bump(s_t, ins)
                    x1_ld[glc] = (x1_ld[glc][0], s_t.v)
                    glc += 1
                V.wait_ge(s_t.h, s_t.v)
                for dc4 in range(4):
                    pt = (pA, pB, pC, pD)[dc4]
                    ins = V.tensor_copy(
                        slotsT_v[:, dh2 * 4 + dc4, 512 * npc:512 * npc + 512], pt[:])
                bump(s_v, ins)
        bnop(s_v, V)
        phase_barrier()

        sb_free(q1_mark)
        wbuf = sb("wbuf", [128, 2 * 4 * 8 * 512], BF16)
        bstage = sb("bstage", [1, 4 * DM], BF16)
        y1st = sb("y1st", [128, 2 * 512], BF16)
        wbuf_v = wbuf[:].rearrange("p (u kx dc e) -> p u kx dc e", u=2, kx=4, dc=8)
        y1st_v = y1st[:].rearrange("p (u e) -> p u e", u=2)
        w_ld = {}
        ev_done = {}
        y1_dma = {0: 0, 1: 0}
        kk = 0
        for eh in range(2):
            for ng in range(16):
                u = kk % 2
                SY.wait_ge(sld.h, sld.v)
                if kk >= 2:
                    SY.wait_ge(s_t.h, w_ld[kk - 2])
                for kx in range(4):
                    n = ng * 4 + kx
                    bump(sld, SY.dma_start(
                        out=wbuf_v[:, u, kx, :, :],
                        in_=wffn[n, :, 512 * eh:512 * eh + 512]
                            .rearrange("(dc p) e -> p dc e", p=128)), 16)
                bump(sld, SY.dma_start(
                    out=bstage[0:1, :],
                    in_=bffn_row[0:1, ng * 4 * DM:(ng + 1) * 4 * DM]), 16)
                wld_v = sld.v
                pt = pA if u == 0 else pB
                T.wait_ge(sld.h, wld_v)
                if kk >= 2:
                    T.wait_ge(s_s.h, ev_done[kk - 2])
                for kx in range(4):
                    for dc in range(8):
                        T.matmul(pt[32 * kx:32 * kx + 32, :],
                                 slotsT_v[:, dc, (ng * 4 + kx) * 32:(ng * 4 + kx) * 32 + 32],
                                 wbuf_v[:, u, kx, dc, :],
                                 start=(dc == 0), stop=False,
                                 tile_position=(0, 32 * kx))
                    ins = T.matmul(pt[32 * kx:32 * kx + 32, :],
                                   onesr_b[0:1, 0:32],
                                   bstage[0:1, kx * DM + 512 * eh:kx * DM + 512 * eh + 512],
                                   start=False, stop=True, tile_position=(0, 32 * kx))
                ins = bump(s_t, ins)
                w_ld[kk] = s_t.v
                SC.wait_ge(s_t.h, s_t.v)
                if kk >= 2:
                    SC.wait_ge(sst.h, y1_dma[u])
                ins = SC.copy(out=y1st_v[:, u, :], in_=pt[:])
                bump(s_s, ins)
                ev_done[kk] = s_s.v
                SY.wait_ge(s_s.h, s_s.v)
                SY.wait_ge(sst.h, sst.v)
                bump(sst, SY.dma_start(
                    out=y2_sp[:, 512 * eh:512 * eh + 512]
                        .rearrange("(ng p) e -> p ng e", p=128)[:, ng, :],
                    in_=y1st_v[:, u, :]), 16)
                y1_dma[u] = sst.v
                kk += 1
        phase_barrier()

        sb_free(slots_mark)
        y1blk = sb("y1blk", [128, 2 * 16 * 128], BF16)
        eT_st = sb("eT_st", [128, 2 * 16 * 512], BF16)
        r1st = sb("r1st", [128, 2 * 512], F32)
        ost = sb("ost", [128, 2 * 512], F32)
        y1b_v = y1blk[:].rearrange("p (u nc e) -> p u nc e", u=2, nc=16)
        eT_v = eT_st[:].rearrange("p (u nc m) -> p u nc m", u=2, nc=16)
        r1_vv = r1st[:].rearrange("p (u m) -> p u m", u=2)
        ost_v = ost[:].rearrange("p (u m) -> p u m", u=2)
        x3T_v = x3T.rearrange("(ec p) m -> p ec m", p=128)
        outT_v = out_T.rearrange("(ec p) m -> p ec m", p=128)
        eT_ld = {}
        o_done = {0: 0, 1: 0}
        o_dma = {0: 0, 1: 0}
        r1_used = {0: 0, 1: 0}
        y1b_used = {0: 0, 1: 0}
        for mc4 in range(NM4):
            u4 = mc4 % 2
            SY.wait_ge(sld.h, sld.v)
            if mc4 >= 2:
                SY.wait_ge(s_t.h, eT_ld[mc4 - 2])
            bump(sld, SY.dma_start(
                out=eT_v[:, u4, :, :],
                in_=elTsc[:, 512 * mc4:512 * mc4 + 512]
                    .rearrange("(nc p) m -> p nc m", p=128)), 16)
            eTld_v = sld.v
            for ec in range(8):
                ue = ec % 2
                SY.wait_ge(sld.h, sld.v)
                if mc4 > 0 or ec >= 2:
                    SY.wait_ge(s_v.h, r1_used[ue])
                    SY.wait_ge(s_t.h, y1b_used[ue])
                bump(sld, SY.dma_start(out=r1_vv[:, ue, :],
                                       in_=x3T_v[:, ec, 512 * mc4:512 * mc4 + 512]), 16)
                bump(sld, SY.dma_start(
                    out=y1b_v[:, ue, :, :],
                    in_=y2_sp[:, 128 * ec:128 * ec + 128]
                        .rearrange("(nc p) e -> p nc e", p=128)), 16)
                r1ld_v = sld.v
                pt = pA if ue == 0 else pB
                T.wait_ge(sld.h, eTld_v)
                T.wait_ge(sld.h, r1ld_v)
                if mc4 > 0 or ec >= 2:
                    T.wait_ge(s_v.h, o_done[ue])
                for nco in range(16):
                    ins = T.matmul(pt[:], y1b_v[:, ue, nco, :],
                                   eT_v[:, u4, nco, :],
                                   start=(nco == 0), stop=(nco == 15))
                ins = bump(s_t, ins)
                if ec == 7:
                    eT_ld[mc4] = s_t.v
                y1b_used[ue] = s_t.v
                V.wait_ge(s_t.h, s_t.v)
                V.wait_ge(sld.h, r1ld_v)
                if mc4 > 0 or ec >= 2:
                    V.wait_ge(sst.h, o_dma[ue])
                ins = V.tensor_tensor(out=ost_v[:, ue, :], in0=pt[:],
                                      in1=r1_vv[:, ue, :], op=ALU.add)
                ins = bnop(s_v, V)
                o_done[ue] = s_v.v
                r1_used[ue] = s_v.v
                SY.wait_ge(s_v.h, s_v.v)
                SY.wait_ge(sst.h, sst.v)
                bump(sst, SY.dma_start(
                    out=outT_v[:, ec, 512 * mc4:512 * mc4 + 512],
                    in_=ost_v[:, ue, :]), 16)
                o_dma[ue] = sst.v
        SY.wait_ge(sst.h, sst.v)

    for cm in reversed(ctx):
        cm.__exit__(None, None, None)
    return nc



def _bf(a):
    return np.ascontiguousarray(np.asarray(a)).astype(ml_dtypes.bfloat16)


def _prep_consts():
    mask8 = np.zeros((8, 128), np.float32)
    for p in range(128):
        mask8[p % 8, p] = 1.0
    maskf = np.zeros((128, 8), np.float32)
    for p in range(128):
        maskf[p, p % 8] = 1.0
    return {
        "mask8": _bf(mask8),
        "maskf": maskf,
        "onesr": np.ones((1, 128), np.float32),
        "onesc_b": _bf(np.ones((128, 1), np.float32)),
        "sel16c": np.stack([np.arange(128) // 8 == s for s in range(16)]).astype(np.float32),
        "identf": np.eye(128, dtype=np.float32),
        "identb": _bf(np.eye(128, dtype=np.float32)),
        "onesrb": _bf(np.ones((1, 128), np.float32)),
    }


def make_L1_inputs(inputs, core):
    c = _prep_consts()
    x = np.asarray(inputs["L0_attn_a_x"])
    return {
        "xT": np.ascontiguousarray(np.asarray(x[core]).T).astype(np.float32),
        "w_inproj": _bf(np.asarray(inputs["L0_attn_a_in_proj_w"]).T),
        "w_conv": np.ascontiguousarray(np.asarray(inputs["L0_attn_a_conv_w"])[:, 0, :]).astype(np.float32),
        "conv_b": np.asarray(inputs["L0_attn_a_conv_b"]).astype(np.float32).reshape(DI, 1),
        "w_xproj": _bf(np.asarray(inputs["L0_attn_a_x_proj_w"]).T),
        "w_dtw": np.asarray(inputs["L0_attn_a_dt_proj_w"]).astype(np.float32).reshape(DI, 1),
        "dt_b": np.asarray(inputs["L0_attn_a_dt_proj_b"]).astype(np.float32).reshape(DI, 1),
        "A_log": np.asarray(inputs["L0_attn_a_A_log"]).astype(np.float32).reshape(DS, 1),
        "D_vec": np.asarray(inputs["L0_attn_a_D"]).astype(np.float32).reshape(DI, 1),
        "w_outproj": _bf(np.asarray(inputs["L0_attn_a_out_proj_w"]).T),
        "ln_g": np.asarray(inputs["L0_attn_a_ln_g"]).astype(np.float32).reshape(DI, 1),
        "ln_b": np.asarray(inputs["L0_attn_a_ln_b"]).astype(np.float32).reshape(DI, 1),
        "phi": _bf(np.asarray(inputs["L0_ffn_phi"]).reshape(DM, NP)),
        **c,
    }


def _l2_consts():
    diagm = np.zeros((128, 4 * 512), np.float32)
    for v in range(4):
        for i in range(128):
            diagm[i, 512 * v:512 * (v + 1)] = (i + 128 * v) <= np.arange(512)
    return {
        "diagm": _bf(diagm),
        "identb": _bf(np.eye(128, dtype=np.float32)),
        "onesrb": _bf(np.ones((1, 128), np.float32)),
        "ones512b": _bf(np.ones((1, 512), np.float32)),
        "onesrf": np.ones((1, 128), np.float32),
        "onescb": _bf(np.ones((128, 1), np.float32)),
    }


def make_L2_inputs(inputs, core, l1outs_all):
    own = l1outs_all[core]
    d = {
        "r1T": np.ascontiguousarray(own["r1"]),
        "x1lntok": np.ascontiguousarray(own["x1lntok"]),
        "elTsc": np.ascontiguousarray(own["el1Tsc"]),
        "wffn": _bf(np.asarray(inputs["L0_ffn_W"])),
        "bffn_row": _bf(np.asarray(inputs["L0_ffn_b"]).reshape(1, NEXP * DM)),
        "wqkv": _bf(np.asarray(inputs["L0_attn_b_Wqkv"])),
        "bqkv_col": np.asarray(inputs["L0_attn_b_bqkv"]).astype(np.float32).reshape(3 * DM, 1),
        "bv_row": _bf(np.asarray(inputs["L0_attn_b_bqkv"])[2 * DM:].reshape(1, DM)),
        "wout": _bf(np.asarray(inputs["L0_attn_b_Wout"])),
        "bout_row": _bf(np.asarray(inputs["L0_attn_b_bout"]).reshape(1, DM)),
        "phi": _bf(np.asarray(inputs["L0_ffn_phi"]).reshape(DM, NP)),
        **_l2_consts(),
    }
    order = [core] + [b for b in range(8) if b != core]
    for i, b in enumerate(order):
        d[f"el_in{i}"] = np.ascontiguousarray(l1outs_all[b]["el1"])
    return d


def make_L3_inputs(inputs, core, l2outs_all):
    own = l2outs_all[core]
    d = {
        "x3T": np.ascontiguousarray(own["x3T"]),
        "x3lnT": np.ascontiguousarray(own["x3lnT"]),
        "elTsc": np.ascontiguousarray(own["el2Tsc"]),
        "wffn": _bf(np.asarray(inputs["L0_ffn_W"])),
        "bffn_row": _bf(np.asarray(inputs["L0_ffn_b"]).reshape(1, NEXP * DM)),
        "identb": _bf(np.eye(128, dtype=np.float32)),
        "onesrb": _bf(np.ones((1, 128), np.float32)),
    }
    order = [core] + [b for b in range(8) if b != core]
    for i, b in enumerate(order):
        d[f"el_in{i}"] = np.ascontiguousarray(l2outs_all[b]["el2"])
    return d


def kernel(**inputs):
    nc1 = build_L1()
    l1_in = [make_L1_inputs(inputs, core) for core in range(8)]
    res1 = run_bass_kernel_spmd(nc1, l1_in, core_ids=list(range(8)))
    l1outs = [res1.results[c] for c in range(8)]

    nc2 = build_L2()
    l2_in = [make_L2_inputs(inputs, core, l1outs) for core in range(8)]
    res2 = run_bass_kernel_spmd(nc2, l2_in, core_ids=list(range(8)))
    l2outs = [res2.results[c] for c in range(8)]

    nc3 = build_L3()
    l3_in = [make_L3_inputs(inputs, core, l2outs) for core in range(8)]
    res3 = run_bass_kernel_spmd(nc3, l3_in, core_ids=list(range(8)))

    out = np.stack([np.asarray(res3.results[c]["outT"]).T for c in range(8)])
    return np.ascontiguousarray(out.astype(np.float32))
